# version 16
# baseline (speedup 1.0000x reference)
import numpy as np
import ml_dtypes
from contextlib import ExitStack
import concourse.bass as bass
import concourse.mybir as mybir

F32 = mybir.dt.float32
BF16 = mybir.dt.bfloat16
FP8 = mybir.dt.float8e4
AF = mybir.ActivationFunctionType
ALU = mybir.AluOpType
AXL = mybir.AxisListType
DR = mybir.MatmulPerfMode.DoubleRow

S = 2048
DM = 2048
H = 16
NT = 16
NC = 16
NQ = 4
QH = 16
INVSQRT_HD = 0.08838834764831845
C1 = 0.796875
C2 = 0.0446777344


class Sems:
    def __init__(self, nc, es):
        self.nc = nc; self.es = es; self.h = {}; self.n = {}
    def get(self, name):
        if name not in self.h:
            self.h[name] = self.es.enter_context(self.nc.semaphore(name))
            self.n[name] = 0
        return self.h[name]
    def inc(self, instr, name, k=1):
        instr.then_inc(self.get(name), k)
        self.n[name] += k
        return self.n[name]
    def val(self, name):
        self.get(name)
        return self.n[name]


def score_tiles():
    out = []
    i = 0
    for Bi in range(4):
        for J in range(4 * Bi + 4):
            out.append((i, Bi, J, J * 128 - Bi * 512))
            i += 1
    return out


def build_nc(phases=("A", "D", "B", "C"), debug=(), d_skip=()):
    nc = bass.Bass("TRN2", target_bir_lowering=False, detect_race_conditions=False)
    es = ExitStack()
    sm = Sems(nc, es)

    x_d   = nc.dram_tensor("x_d",   [S, DM], F32, kind="ExternalInput")
    wq_d  = nc.dram_tensor("wq_d",  [H * 128, DM], F32, kind="ExternalInput")
    wk_d  = nc.dram_tensor("wk_d",  [H * 128, DM], F32, kind="ExternalInput")
    wv_d  = nc.dram_tensor("wv_d",  [H * 128, DM], F32, kind="ExternalInput")
    wo_d  = nc.dram_tensor("wo_d",  [NC * 128, DM], F32, kind="ExternalInput")
    wg0_d = nc.dram_tensor("wg0_d", [64 * 128, DM], F32, kind="ExternalInput")
    wg1_d = nc.dram_tensor("wg1_d", [64 * 128, DM], F32, kind="ExternalInput")
    wl_d  = nc.dram_tensor("wl_d",  [64 * 128, DM], F32, kind="ExternalInput")
    idf_d = nc.dram_tensor("idf_d", [128, 128], F32, kind="ExternalInput")
    idb_d = nc.dram_tensor("idb_d", [128, 128], BF16, kind="ExternalInput")
    msk_d = nc.dram_tensor("msk_d", [128, 128], BF16, kind="ExternalInput")
    eps_d = nc.dram_tensor("eps_d", [128, 1], F32, kind="ExternalInput")

    out_d = nc.dram_tensor("out_d", [S, DM], F32, kind="ExternalOutput")
    xn_h  = nc.dram_tensor("xn_h",  [NC * 128, S], BF16)
    y_h   = nc.dram_tensor("y_h",   [S, DM], F32)

    dbg = {}
    if "xnqx" in debug:
        dbg["xn_o"] = nc.dram_tensor("xn_o", [NC * 128, S], BF16, kind="ExternalOutput")
        dbg["qx_o"] = nc.dram_tensor("qx_o", [NC * 128, S], FP8, kind="ExternalOutput")
    if "y" in debug:
        dbg["y_o"] = nc.dram_tensor("y_o", [S, DM], F32, kind="ExternalOutput")
    if "stats" in debug:
        dbg["st_o"] = nc.dram_tensor("st_o", [128, 4 * NT], F32, kind="ExternalOutput")
    if "c" in debug:
        dbg["c_o"] = nc.dram_tensor("c_o", [128, NT * DM], BF16, kind="ExternalOutput")
        dbg["r_o"] = nc.dram_tensor("r_o", [128, H * NT], F32, kind="ExternalOutput")

    def mk_sb(stack):
        def f(name, shape, dt):
            return stack.enter_context(nc.sbuf_tensor(name, shape, dt))
        return f

    SBg = mk_sb(es)
    idf    = SBg("idf", [128, 128], F32)
    idb    = SBg("idb", [128, 128], BF16)
    msk    = SBg("msk", [128, 128], BF16)
    eps_ap = SBg("eps_ap", [128, 1], F32)

    with nc.Block(no_gpsimd_drain=True) as blk:
        @blk.sync
        def _(sync):
            sm.inc(sync.dma_start(idf[:, :], idf_d[:, :]), "misc", 16)
            sm.inc(sync.dma_start(idb[:, :], idb_d[:, :]), "misc", 16)
            sm.inc(sync.dma_start(msk[:, :], msk_d[:, :]), "misc", 16)
            sm.inc(sync.dma_start(eps_ap[:, :], eps_d[:, :]), "misc", 16)
            sync.wait_ge(sm.get("misc"), sm.val("misc"))

    esAD = ExitStack()
    SAD = mk_sb(esAD)
    qx_sb = SAD("qx_sb", [128, NC * S], FP8)
    wg0_t = [SAD(f"wg0_t{r}", [128, DM], FP8) for r in range(3)]
    wg1_t = [SAD(f"wg1_t{r}", [128, DM], FP8) for r in range(3)]

    if "A" in phases:
        esA = ExitStack()
        SA = mk_sb(esA)
        xbuf   = [SA(f"xbuf{i}", [128, DM], F32) for i in range(8)]
        xn_sbA = SA("xn_sbA", [128, NC * S], BF16)
        xntok  = [SA(f"xntok{r}", [128, DM], F32) for r in range(3)]
        sqscr  = SA("sqscr", [128, DM], F32)
        sumx   = SA("sumx", [128, NT], F32)
        sumsq  = SA("sumsq", [128, NT], F32)
        mm_    = SA("mm_", [128, NT], F32)
        vv_    = SA("vv_", [128, NT], F32)
        msq_   = SA("msq_", [128, NT], F32)
        vvm_   = SA("vvm_", [128, NT], F32)
        sd_    = SA("sd_", [128, NT], F32)
        rs_    = SA("rs_", [128, NT], F32)
        PT = [esA.enter_context(nc.psum_tensor(f"PTa{r}", [128, 128], F32)) for r in range(4)]

        with nc.Block(no_gpsimd_drain=True) as blk:
            @blk.sync
            def _(sync):
                for t in range(0, NT, 2):
                    i = t % 8
                    if t >= 8:
                        sync.wait_ge(sm.get("s_n"), t - 8 + 1)
                    sm.inc(sync.dma_start(xbuf[i][:, :], x_d[t * 128:(t + 1) * 128, :]), f"s_xb{i}", 16)
                sync.wait_ge(sm.get("s_xnb"), 256)
                if "stats" in debug:
                    for (bi, buf) in enumerate((sumx, sumsq, mm_, rs_)):
                        sm.inc(sync.dma_start(dbg["st_o"][:, bi * NT:(bi + 1) * NT], buf[:, :]), "s_dbgs", 16)
                for c in range(NC):
                    sm.inc(sync.dma_start(xn_h[c * 128:(c + 1) * 128, :], xn_sbA[:, c * S:(c + 1) * S]), "s_xn_out", 16)
                if "xnqx" in debug:
                    for c in range(NC):
                        sm.inc(sync.dma_start(dbg["xn_o"][c * 128:(c + 1) * 128, :], xn_sbA[:, c * S:(c + 1) * S]), "s_dbg", 16)
                    sync.wait_ge(sm.get("s_qx"), 256)
                    for c in range(NC):
                        sm.inc(sync.dma_start(dbg["qx_o"][c * 128:(c + 1) * 128, :], qx_sb[:, c * S:(c + 1) * S]), "s_dbg", 16)
                    sync.wait_ge(sm.get("s_dbg"), sm.val("s_dbg"))

            @blk.gpsimd
            def _(gpsimd):
                if "D" in phases:
                    for ht in range(3):
                        sm.inc(gpsimd.dma_start(wg0_t[ht % 3][:, :], wg0_d[ht * 128:(ht + 1) * 128, :]), f"s_wg0_{ht % 3}", 16)
                        sm.inc(gpsimd.dma_start(wg1_t[ht % 3][:, :], wg1_d[ht * 128:(ht + 1) * 128, :]), f"s_wg1_{ht % 3}", 16)

            @blk.vector
            def _(vector):
                def xn_evac(j):
                    for c in range(NC):
                        vector.wait_ge(sm.get("s_ta"), j * 16 + c + 1)
                        sm.inc(vector.tensor_copy(xn_sbA[:, c * S + j * 128: c * S + (j + 1) * 128],
                                                  PT[(j * 16 + c) % 4][:, :]), "s_xnb", 1)
                def norm(t, k):
                    if t >= 3:
                        vector.wait_ge(sm.get("s_ta"), (t - 2) * 16)
                    sm.inc(vector.tensor_scalar(xntok[t % 3][:, :], xbuf[k][:, :],
                                                mm_[:, t:t + 1], rs_[:, t:t + 1],
                                                op0=ALU.subtract, op1=ALU.mult), "s_n", 1)
                for b in range(2):
                    for k in range(8):
                        t = b * 8 + k
                        vector.wait_ge(sm.get(f"s_xb{k}"), 16 * (b + 1))
                        sm.inc(vector.reduce_sum(sumx[:, t:t + 1], xbuf[k][:, :], axis=AXL.X), "s_v1", 1)
                    cols = slice(b * 8, b * 8 + 8)
                    vector.wait_ge(sm.get("s_s1"), 8 * (b + 1))
                    vector.wait_ge(sm.get("s_v1"), 8 * (b + 1))
                    sm.inc(vector.tensor_scalar_mul(mm_[:, cols], sumx[:, cols], 1.0 / DM), "s_v2", 1)
                    sm.inc(vector.tensor_scalar_mul(vv_[:, cols], sumsq[:, cols], 1.0 / DM), "s_v5", 1)
                    vector.wait_ge(sm.get("s_s2"), b + 1)
                    vector.wait_ge(sm.get("s_v5"), b + 1)
                    sm.inc(vector.tensor_sub(vvm_[:, cols], vv_[:, cols], msq_[:, cols]), "s_v3", 1)
                    vector.wait_ge(sm.get("s_s3"), b + 1)
                    sm.inc(vector.reciprocal(rs_[:, cols], sd_[:, cols]), "s_v4", 1)
                    vector.wait_ge(sm.get("s_v4"), b + 1)
                    for k in range(8):
                        t = b * 8 + k
                        if t >= 1:
                            xn_evac(t - 1)
                        norm(t, k)
                xn_evac(NT - 1)

            @blk.scalar
            def _(scalar):
                def qx_evac(j):
                    for c in range(NC):
                        scalar.wait_ge(sm.get("s_ta"), j * 16 + c + 1)
                        scalar.wait_ge(sm.get("s_xnb"), j * 16 + c + 1)
                        sm.inc(scalar.copy(qx_sb[:, c * S + j * 128: c * S + (j + 1) * 128],
                                           PT[(j * 16 + c) % 4][:, :]), "s_qx", 1)
                def stats(b):
                    for k in range(8):
                        t = b * 8 + k
                        scalar.wait_ge(sm.get(f"s_xb{k}"), 16 * (b + 1))
                        sm.inc(scalar.activation(sqscr[:, :], xbuf[k][:, :], AF.Square,
                                                 accum_out=sumsq[:, t:t + 1]), "s_s1", 1)
                    cols = slice(b * 8, b * 8 + 8)
                    scalar.wait_ge(sm.get("s_v2"), b + 1)
                    sm.inc(scalar.activation(msq_[:, cols], mm_[:, cols], AF.Square), "s_s2", 1)
                    scalar.wait_ge(sm.get("s_v3"), b + 1)
                    sm.inc(scalar.activation(sd_[:, cols], vvm_[:, cols], AF.Sqrt, bias=eps_ap[:, :]), "s_s3", 1)
                for t in range(1, 8, 2):
                    sm.inc(scalar.dma_start(xbuf[t][:, :], x_d[t * 128:(t + 1) * 128, :]), f"s_xb{t}", 16)
                stats(0)
                for j in range(0, 7):
                    qx_evac(j)
                for t in range(9, NT, 2):
                    scalar.wait_ge(sm.get("s_n"), t - 8 + 1)
                    sm.inc(scalar.dma_start(xbuf[t % 8][:, :], x_d[t * 128:(t + 1) * 128, :]), f"s_xb{t % 8}", 16)
                stats(1)
                for j in range(7, NT):
                    qx_evac(j)

            @blk.tensor
            def _(tensor):
                tensor.wait_ge(sm.get("misc"), 64)
                for t in range(NT):
                    tensor.wait_ge(sm.get("s_n"), t + 1)
                    for c in range(NC):
                        k = t * 16 + c
                        if k >= 4:
                            tensor.wait_ge(sm.get("s_xnb"), k - 3)
                            tensor.wait_ge(sm.get("s_qx"), k - 3)
                        sm.inc(tensor.transpose(PT[k % 4][:, :], xntok[t % 3][:, c * 128:(c + 1) * 128],
                                                idf[:, :]), "s_ta", 1)
        esA.close()

    if "D" in phases:
        esD = ExitStack()
        SD = mk_sb(esD)
        act_sb = SD("act_sb", [128, QH * S], FP8)
        wl_sb  = SD("wl_sb", [128, QH * S], FP8)
        g_sb   = SD("g_sb", [128, DM], BF16)
        f1_sb  = SD("f1_sb", [128, DM], BF16)
        e1     = SD("e1", [128, DM], BF16)
        e2     = SD("e2", [128, DM], BF16)
        gel    = SD("gel", [128, DM], BF16)
        xy     = [SD(f"xy{r}", [128, DM], F32) for r in range(2)]
        yo     = [SD(f"yo{r}", [128, DM], F32) for r in range(2)]
        PA = esD.enter_context(nc.psum_tensor("PA", [128, DM], F32))
        PB = esD.enter_context(nc.psum_tensor("PB", [128, DM], F32))

        with nc.Block(no_gpsimd_drain=True) as blk:
            @blk.sync
            def _(sync):
                if "A" in phases:
                    sync.wait_ge(sm.get("s_xn_out"), 256)
                for q in range(NQ):
                    for i in range(NT):
                        if "d2" in d_skip:
                            break
                        n = q * NT + i
                        if n >= 2:
                            sync.wait_ge(sm.get("s_ya"), 4 * (n - 1))
                        src = x_d if q == 0 else y_h
                        sm.inc(sync.dma_start(xy[i % 2][:, :], src[i * 128:(i + 1) * 128, :]), f"s_xy{n % 2}", 16)
                        sync.wait_ge(sm.get("s_ya"), 4 * (n + 1))
                        sm.inc(sync.dma_start(y_h[i * 128:(i + 1) * 128, :], yo[i % 2][:, :]), f"s_yw{n % 2}", 16)
                if "y" in debug and "d2" not in d_skip:
                    sync.wait_ge(sm.get("s_yw0"), 16 * NQ * NT // 2)
                    sync.wait_ge(sm.get("s_yw1"), 16 * NQ * NT // 2)
                    for i in range(NT):
                        sm.inc(sync.dma_start(dbg["y_o"][i * 128:(i + 1) * 128, :], y_h[i * 128:(i + 1) * 128, :]), "s_dbg2", 16)
                    sync.wait_ge(sm.get("s_dbg2"), sm.val("s_dbg2"))
                if "C" not in phases:
                    if "d2" not in d_skip:
                        sync.wait_ge(sm.get("s_yw0"), 16 * NQ * NT // 2)
                        sync.wait_ge(sm.get("s_yw1"), 16 * NQ * NT // 2)
                    for i in range(NT):
                        sm.inc(sync.dma_start(out_d[i * 128:(i + 1) * 128, :], y_h[i * 128:(i + 1) * 128, :]), "s_fin", 16)
                    sync.wait_ge(sm.get("s_fin"), sm.val("s_fin"))

            @blk.gpsimd
            def _(gpsimd):
                if "A" in phases:
                    gpsimd.wait_ge(sm.get("s_xn_out"), 256)
                pre = 3 if "A" in phases else 0
                def wg_load(n):
                    if n >= pre and n < NQ * QH:
                        if n >= 3:
                            gpsimd.wait_ge(sm.get("s_gf"), 2 * (n - 2))
                        sm.inc(gpsimd.dma_start(wg0_t[n % 3][:, :], wg0_d[n * 128:(n + 1) * 128, :]), f"s_wg0_{n % 3}", 16)
                        sm.inc(gpsimd.dma_start(wg1_t[n % 3][:, :], wg1_d[n * 128:(n + 1) * 128, :]), f"s_wg1_{n % 3}", 16)
                for q in range(NQ):
                    for ht in range(QH):
                        wg_load(q * QH + ht)
                        if "d2" not in d_skip:
                            row = q * QH + ht
                            if q >= 1:
                                gpsimd.wait_ge(sm.get("s_d2"), 4 * NT * q)
                            sm.inc(gpsimd.dma_start(wl_sb[:, ht * S:(ht + 1) * S],
                                                    wl_d[row * 128:(row + 1) * 128, :]), f"s_wl{q % 2}", 16)

            @blk.tensor
            def _(tensor):
                if "A" in phases:
                    tensor.wait_ge(sm.get("s_qx"), 256)
                for q in range(NQ):
                    if q >= 1:
                        tensor.wait_ge(sm.get("s_ya"), 64 * q - 4)
                    for ht in range(QH):
                        n = q * QH + ht
                        tensor.wait_ge(sm.get(f"s_wg0_{n % 3}"), 16 * (n // 3 + 1))
                        tensor.wait_ge(sm.get(f"s_wg1_{n % 3}"), 16 * (n // 3 + 1))
                        if n >= 1:
                            tensor.wait_ge(sm.get("s_gev"), 2 * n - 1)
                        for tc in range(4):
                            for c2 in range(NC // 2):
                                mi = tensor.matmul(PA[:, tc * 512:(tc + 1) * 512],
                                              bass.AP(wg0_t[n % 3], 2 * c2 * 128, [[S, 128], [128, 2], [1, 128]]),
                                              bass.AP(qx_sb, 2 * c2 * S + tc * 512, [[NC * S, 128], [S, 2], [1, 512]]),
                                              start=(c2 == 0), stop=(c2 == NC // 2 - 1), perf_mode=DR)
                        sm.inc(mi, "s_gf", 1)
                        if n >= 1:
                            tensor.wait_ge(sm.get("s_gev"), 2 * n)
                        if q >= 1 and ht == 0:
                            tensor.wait_ge(sm.get("s_ya"), 64 * q)
                        for tc in range(4):
                            for c2 in range(NC // 2):
                                mi = tensor.matmul(PB[:, tc * 512:(tc + 1) * 512],
                                              bass.AP(wg1_t[n % 3], 2 * c2 * 128, [[S, 128], [128, 2], [1, 128]]),
                                              bass.AP(qx_sb, 2 * c2 * S + tc * 512, [[NC * S, 128], [S, 2], [1, 512]]),
                                              start=(c2 == 0), stop=(c2 == NC // 2 - 1), perf_mode=DR)
                        sm.inc(mi, "s_gf", 1)
                    if "d2" in d_skip:
                        continue
                    tensor.wait_ge(sm.get("s_act"), (q + 1) * QH)
                    tensor.wait_ge(sm.get(f"s_wl{q % 2}"), 16 * QH * (q // 2 + 1))
                    tensor.wait_ge(sm.get("s_gev"), 2 * QH * (q + 1))
                    for i in range(NT):
                        n2 = q * NT + i
                        P = PA if i % 2 == 0 else PB
                        if n2 >= 2:
                            tensor.wait_ge(sm.get("s_ya"), 4 * (n2 - 1))
                        for fc in range(4):
                            for h2 in range(QH // 2):
                                mi = tensor.matmul(P[:, fc * 512:(fc + 1) * 512],
                                              bass.AP(act_sb, 2 * h2 * S + i * 128, [[QH * S, 128], [S, 2], [1, 128]]),
                                              bass.AP(wl_sb, 2 * h2 * S + fc * 512, [[QH * S, 128], [S, 2], [1, 512]]),
                                              start=(h2 == 0), stop=(h2 == QH // 2 - 1), perf_mode=DR)
                            sm.inc(mi, "s_d2", 1)

            @blk.vector
            def _(vector):
                for q in range(NQ):
                    if q >= 1 and "d2" not in d_skip:
                        vector.wait_ge(sm.get("s_d2"), 4 * NT * q)
                    for ht in range(QH):
                        n = q * QH + ht
                        vector.wait_ge(sm.get("s_gev"), 2 * n + 1)
                        vector.tensor_mul(e1[:, :], g_sb[:, :], g_sb[:, :])
                        vector.tensor_mul(e2[:, :], e1[:, :], g_sb[:, :])
                        vector.tensor_scalar_mul(e1[:, :], e2[:, :], C2)
                        vector.tensor_add(e2[:, :], g_sb[:, :], e1[:, :])
                        sm.inc(vector.tensor_scalar_mul(e1[:, :], e2[:, :], C1), "s_w", 1)
                        vector.wait_ge(sm.get("s_th"), n + 1)
                        vector.wait_ge(sm.get("s_gev"), 2 * n + 2)
                        vector.tensor_scalar(e2[:, :], gel[:, :], 1.0, 0.5, op0=ALU.add, op1=ALU.mult)
                        vector.tensor_mul(e1[:, :], g_sb[:, :], e2[:, :])
                        vector.tensor_mul(e2[:, :], e1[:, :], f1_sb[:, :])
                        sm.inc(vector.tensor_copy(act_sb[:, ht * S:(ht + 1) * S], e2[:, :]), "s_act", 1)
                    for i in range(NT):
                        if "d2" in d_skip:
                            break
                        n2 = q * NT + i
                        P = PA if i % 2 == 0 else PB
                        vector.wait_ge(sm.get(f"s_xy{n2 % 2}"), 16 * (n2 // 2 + 1))
                        if n2 >= 2:
                            vector.wait_ge(sm.get(f"s_yw{n2 % 2}"), 16 * (n2 // 2))
                        for fc in range(4):
                            vector.wait_ge(sm.get("s_d2"), 4 * n2 + fc + 1)
                            sm.inc(vector.tensor_add(yo[i % 2][:, fc * 512:(fc + 1) * 512],
                                                     P[:, fc * 512:(fc + 1) * 512],
                                                     xy[i % 2][:, fc * 512:(fc + 1) * 512]), "s_ya", 1)

            @blk.scalar
            def _(scalar):
                for q in range(NQ):
                    for ht in range(QH):
                        n = q * QH + ht
                        scalar.wait_ge(sm.get("s_gf"), 2 * n + 1)
                        if n >= 1:
                            scalar.wait_ge(sm.get("s_act"), n)
                        sm.inc(scalar.copy(g_sb[:, :], PA[:, :]), "s_gev", 1)
                        scalar.wait_ge(sm.get("s_gf"), 2 * n + 2)
                        sm.inc(scalar.copy(f1_sb[:, :], PB[:, :]), "s_gev", 1)
                        scalar.wait_ge(sm.get("s_w"), n + 1)
                        sm.inc(scalar.activation(gel[:, :], e1[:, :], AF.Tanh), "s_th", 1)
        esD.close()
    esAD.close()

    if "B" in phases or "C" in phases:
        esBC = ExitStack()
        SBC = mk_sb(esBC)
        c_sb = SBC("c_sb", [128, NT * DM], BF16)
        r_sb = SBC("r_sb", [128, H * NT], F32)

    if "B" in phases:
        esB = ExitStack()
        SBB = mk_sb(esB)
        xn_sbB = SBB("xn_sbB", [128, NC * S], BF16)
        q_sbh  = [SBB(f"q_sbh{r}", [128, S], BF16) for r in range(2)]
        k_sbh  = [SBB(f"k_sbh{r}", [128, S], BF16) for r in range(2)]
        vt_sbh = [SBB(f"vt_sbh{r}", [128, S], BF16) for r in range(2)]
        v_sbh  = [SBB(f"v_sbh{r}", [128, NT * 192], BF16) for r in range(2)]
        pt_sb  = SBB("pt_sb", [128, NT * 512], BF16)
        wqt = [SBB(f"wqt{r}", [128, S], BF16) for r in range(2)]
        wkt = [SBB(f"wkt{r}", [128, S], BF16) for r in range(2)]
        wvt = [SBB(f"wvt{r}", [128, S], BF16) for r in range(2)]
        lnS = SBB("lnS", [128, NT], F32)
        PP  = [esB.enter_context(nc.psum_tensor(f"PPb{r}", [128, 512], F32)) for r in range(2)]
        PS_ = [esB.enter_context(nc.psum_tensor(f"PSb{r}", [128, 1024], F32)) for r in range(2)]
        PC_ = [esB.enter_context(nc.psum_tensor(f"PCb{r}", [128, 132], F32)) for r in range(2)]
        

        ST = score_tiles()
        NSC = len(ST)
        CPH = 16

        with nc.Block(no_gpsimd_drain=True) as blk:
            @blk.sync
            def _(sync):
                for c in range(NC):
                    sm.inc(sync.dma_start(xn_sbB[:, c * S:(c + 1) * S], xn_h[c * 128:(c + 1) * 128, :]), f"s_xni{c // 4}", 16)
                for h in range(H):
                    sync.wait_ge(sm.get("s_pe"), 12 * (h + 1))
                    if h >= 2:
                        sync.wait_ge(sm.get("s_cm"), (h - 1) * CPH)
                    for jt in range(NT):
                        sm.inc(sync.dma_start_transpose(v_sbh[h % 2][:, jt * 192: jt * 192 + 128],
                                                        vt_sbh[h % 2][:, jt * 128:(jt + 1) * 128]), f"s_vtt{h % 2}", 16)
                if "c" in debug:
                    sync.wait_ge(sm.get("s_r"), H)
                    sync.wait_ge(sm.get("s_cev"), 2 * H * NT)
                    sm.inc(sync.dma_start(dbg["c_o"][:, :], c_sb[:, :]), "s_dbg3", 16)
                    sm.inc(sync.dma_start(dbg["r_o"][:, :], r_sb[:, :]), "s_dbg3", 16)
                    sync.wait_ge(sm.get("s_dbg3"), sm.val("s_dbg3"))
                if "C" not in phases:
                    sync.wait_ge(sm.get("s_r"), H)
                    for i in range(NT):
                        sm.inc(sync.dma_start(out_d[i * 128:(i + 1) * 128, :], y_h[i * 128:(i + 1) * 128, :]), "s_fin", 16)
                    sync.wait_ge(sm.get("s_fin"), sm.val("s_fin"))

            @blk.gpsimd
            def _(gpsimd):
                for h in range(H):
                    if h >= 2:
                        gpsimd.wait_ge(sm.get("s_pj"), 12 * (h - 1))
                    sm.inc(gpsimd.dma_start(wqt[h % 2][:, :], wq_d[h * 128:(h + 1) * 128, :]), f"s_wq{h % 2}", 16)
                    sm.inc(gpsimd.dma_start(wkt[h % 2][:, :], wk_d[h * 128:(h + 1) * 128, :]), f"s_wk{h % 2}", 16)
                    sm.inc(gpsimd.dma_start(wvt[h % 2][:, :], wv_d[h * 128:(h + 1) * 128, :]), f"s_wv{h % 2}", 16)

            @blk.tensor
            def _(tensor):
                tensor.wait_ge(sm.get("misc"), 64)
                tensor.wait_ge(sm.get("s_vone"), 2)
                nsc = 0; ncc = 0; npj = [0]

                def proj_chain(hh, k):
                    pi, tc = divmod(k, 4)
                    wt = (wqt, wkt, wvt)[pi]
                    if k == 0:
                        tensor.wait_ge(sm.get(f"s_wq{hh % 2}"), 16 * (hh // 2 + 1))
                        tensor.wait_ge(sm.get(f"s_wk{hh % 2}"), 16 * (hh // 2 + 1))
                        tensor.wait_ge(sm.get(f"s_wv{hh % 2}"), 16 * (hh // 2 + 1))
                    if npj[0] >= 2:
                        tensor.wait_ge(sm.get("s_pe"), npj[0] - 1)
                    for c in range(NC):
                        if npj[0] == 0:
                            tensor.wait_ge(sm.get(f"s_xni{c // 4}"), 64)
                        mi = tensor.matmul(PP[npj[0] % 2][:, :],
                                      wt[hh % 2][:, c * 128:(c + 1) * 128],
                                      xn_sbB[:, c * S + tc * 512: c * S + (tc + 1) * 512],
                                      start=(c == 0), stop=(c == NC - 1))
                    sm.inc(mi, "s_pj", 1)
                    npj[0] += 1

                for k in range(12):
                    proj_chain(0, k)
                npr = 0
                for h in range(H):
                    for Bi in range(4):
                        Jmax = 4 * Bi + 4
                        kpos = {max(1, Jmax * (k + 1) // 4): 3 * Bi + k for k in range(3)}
                        for J in range(Jmax):
                            o = J * 128 - Bi * 512
                            if J % 2 == 0 and npr >= 2:
                                tensor.wait_ge(sm.get("s_exp"), npr - 1)
                            mi = tensor.matmul(PS_[npr % 2][:, (J % 2) * 512: (J % 2) * 512 + 512],
                                          k_sbh[h % 2][:, J * 128:(J + 1) * 128],
                                          q_sbh[h % 2][:, Bi * 512: Bi * 512 + 512],
                                          start=True, stop=True)
                            sm.inc(mi, "s_sc", 1)
                            nsc += 1
                            if J % 2 == 1:
                                npr += 1
                            if h + 1 < H and (J + 1) in kpos:
                                proj_chain(h + 1, kpos[J + 1])
                        tensor.wait_ge(sm.get("s_exp"), npr)
                        tensor.wait_ge(sm.get(f"s_vtt{h % 2}"), 256 * (h // 2 + 1))
                        for t4 in range(4):
                            it = 4 * Bi + t4
                            if ncc >= 2:
                                tensor.wait_ge(sm.get("s_cev"), 2 * (ncc - 1))
                            for J in range(it + 1):
                                mi = tensor.matmul(PC_[ncc % 2][:, 0:129],
                                              pt_sb[:, J * 512 + t4 * 128: J * 512 + (t4 + 1) * 128],
                                              v_sbh[h % 2][:, J * 192: J * 192 + 129],
                                              start=(J == 0), stop=(J == it))
                            sm.inc(mi, "s_cm", 1)
                            ncc += 1

            @blk.vector
            def _(vector):
                for r in range(2):
                    for jt in range(NT):
                        mi = vector.memset(v_sbh[r][:, jt * 192 + 128: jt * 192 + 129], 1.0)
                    sm.inc(mi, "s_vone", 1)
                npj = 0; nvt = 0

                def proj_evac(hh, k):
                    nonlocal npj
                    pi = k // 4
                    dst = (q_sbh, k_sbh, vt_sbh)[pi][hh % 2]
                    if hh >= 2:
                        if k == 0:
                            vector.wait_ge(sm.get("s_sc"), (hh - 1) * NSC)
                        if k == 8:
                            vector.wait_ge(sm.get(f"s_vtt{hh % 2}"), 256 * (hh // 2))
                    vector.wait_ge(sm.get("s_pj"), npj + 1)
                    sm.inc(vector.tensor_copy(dst[:, (k % 4) * 512:(k % 4 + 1) * 512], PP[npj % 2][:, :]), "s_pe", 1)
                    npj += 1

                for k in range(12):
                    proj_evac(0, k)
                nprv = 0
                for h in range(H):
                    for Bi in range(4):
                        Jmax = 4 * Bi + 4
                        kpos = {max(1, Jmax * (k + 1) // 4): 3 * Bi + k for k in range(3)}
                        base = sum(4 * b + 4 for b in range(Bi))
                        for J in range(Jmax):
                            o = J * 128 - Bi * 512
                            if o >= 0:
                                g = h * NSC + base + J
                                vector.wait_ge(sm.get("s_sc"), g + 1)
                                sm.inc(vector.tensor_tensor(PS_[nprv % 2][:, (J % 2) * 512 + o: (J % 2) * 512 + o + 128],
                                                            PS_[nprv % 2][:, (J % 2) * 512 + o: (J % 2) * 512 + o + 128],
                                                            msk[:, :], op=ALU.add), "s_mk", 1)
                            if J % 2 == 1:
                                nprv += 1
                            if h + 1 < H and (J + 1) in kpos:
                                proj_evac(h + 1, kpos[J + 1])

            @blk.scalar
            def _(scalar):
                nmk = 0; nprs = 0
                for h in range(H):
                    for Bi in range(4):
                        if h > 0 or Bi > 0:
                            scalar.wait_ge(sm.get("s_cm"), h * CPH + 4 * Bi)
                        for Jl in range(0, 4 * Bi + 4, 2):
                            zero_after = []
                            for J in (Jl, Jl + 1):
                                o = J * 128 - Bi * 512
                                idx = sum(4 * b + 4 for b in range(Bi)) + J
                                g = h * NSC + idx
                                scalar.wait_ge(sm.get("s_sc"), g + 1)
                                if o >= 0:
                                    nmk += 1
                                    scalar.wait_ge(sm.get("s_mk"), nmk)
                                    if o > 0:
                                        zero_after.append((J, o))
                            sm.inc(scalar.activation(pt_sb[:, Jl * 512: (Jl + 2) * 512],
                                                     PS_[nprs % 2][:, 0:1024], AF.Exp,
                                                     scale=INVSQRT_HD), "s_exp", 1)
                            nprs += 1
                            for (J, o) in zero_after:
                                scalar.memzero(pt_sb[:, J * 512: J * 512 + o])
                            if zero_after:
                                scalar.wait_ge(sm.get("s_exp"), nprs)
                        for t4 in range(4):
                            it = 4 * Bi + t4
                            cidx = h * CPH + it
                            scalar.wait_ge(sm.get("s_cm"), cidx + 1)
                            sm.inc(scalar.activation(lnS[:, it:it + 1], PC_[cidx % 2][:, 128:129], AF.Ln), "s_cev", 1)
                            sm.inc(scalar.copy(c_sb[:, it * DM + h * 128: it * DM + (h + 1) * 128],
                                               PC_[cidx % 2][:, 0:128]), "s_cev", 1)
                    scalar.wait_ge(sm.get("s_cev"), 2 * (h + 1) * NT)
                    sm.inc(scalar.activation(r_sb[:, h * NT:(h + 1) * NT], lnS[:, :], AF.Exp, scale=-1.0), "s_r", 1)
        esB.close()

    if "C" in phases:
        esC = ExitStack()
        SC = mk_sb(esC)
        cTt = [SC(f"cTt{r}", [128, DM], BF16) for r in range(2)]
        wo_sb = SC("wo_sb", [128, NC * S], BF16)
        ys = [SC(f"ys{r}", [128, DM], F32) for r in range(2)]
        os_ = [SC(f"os{r}", [128, DM], F32) for r in range(2)]
        PTC = [esC.enter_context(nc.psum_tensor(f"PTc{r}", [128, 128], BF16)) for r in range(2)]
        PO  = [esC.enter_context(nc.psum_tensor(f"POc{r}", [128, 512], F32)) for r in range(4)]

        with nc.Block(no_gpsimd_drain=True) as blk:
            @blk.sync
            def _(sync):
                for i in range(NT):
                    if i >= 2:
                        sync.wait_ge(sm.get("s_oa"), 4 * (i - 1))
                    sm.inc(sync.dma_start(ys[i % 2][:, :], y_h[i * 128:(i + 1) * 128, :]), f"s_yc{i % 2}", 16)
                    sync.wait_ge(sm.get("s_oa"), 4 * (i + 1))
                    sm.inc(sync.dma_start(out_d[i * 128:(i + 1) * 128, :], os_[i % 2][:, :]), f"s_out{i % 2}", 16)
                sync.wait_ge(sm.get("s_out0"), 16 * NT // 2)
                sync.wait_ge(sm.get("s_out1"), 16 * NT // 2)

            @blk.gpsimd
            def _(gpsimd):
                for dc in range(NC):
                    sm.inc(gpsimd.dma_start(wo_sb[:, dc * S:(dc + 1) * S], wo_d[dc * 128:(dc + 1) * 128, :]), "s_wo", 16)

            @blk.vector
            def _(vector):
                for i in range(NT):
                    for hh in range(H):
                        mi = vector.tensor_scalar_mul(c_sb[:, i * DM + hh * 128: i * DM + (hh + 1) * 128],
                                                      c_sb[:, i * DM + hh * 128: i * DM + (hh + 1) * 128],
                                                      r_sb[:, hh * NT + i: hh * NT + i + 1])
                    sm.inc(mi, "s_csc", 1)
                for i in range(NT):
                    if i >= 2:
                        vector.wait_ge(sm.get("s_po"), 4 * (i - 1))
                    for dc in range(NC):
                        ntc = i * NC + dc
                        vector.wait_ge(sm.get("s_tc"), ntc + 1)
                        sm.inc(vector.tensor_copy(cTt[i % 2][:, dc * 128:(dc + 1) * 128], PTC[ntc % 2][:, :]), "s_cte", 1)
                    for fc in range(4):
                        npo = 4 * i + fc
                        vector.wait_ge(sm.get("s_po"), npo + 1)
                        vector.wait_ge(sm.get(f"s_yc{i % 2}"), 16 * (i // 2 + 1))
                        if i >= 2:
                            vector.wait_ge(sm.get(f"s_out{i % 2}"), 16 * (i // 2))
                        sm.inc(vector.tensor_add(os_[i % 2][:, fc * 512:(fc + 1) * 512],
                                                 PO[npo % 4][:, :], ys[i % 2][:, fc * 512:(fc + 1) * 512]), "s_oa", 1)

            @blk.tensor
            def _(tensor):
                for i in range(NT):
                    tensor.wait_ge(sm.get("s_csc"), i + 1)
                    for dc in range(NC):
                        ntc = i * NC + dc
                        if ntc >= 2:
                            tensor.wait_ge(sm.get("s_cte"), ntc - 1)
                        sm.inc(tensor.transpose(PTC[ntc % 2][:, :],
                                                c_sb[:, i * DM + dc * 128: i * DM + (dc + 1) * 128],
                                                idb[:, :]), "s_tc", 1)
                    tensor.wait_ge(sm.get("s_cte"), NC * (i + 1))
                    if i == 0:
                        tensor.wait_ge(sm.get("s_wo"), 16 * NC)
                    for fc in range(4):
                        npo = 4 * i + fc
                        if npo >= 4:
                            tensor.wait_ge(sm.get("s_oa"), npo - 3)
                        for dc in range(NC):
                            mi = tensor.matmul(PO[npo % 4][:, :],
                                          cTt[i % 2][:, dc * 128:(dc + 1) * 128],
                                          wo_sb[:, dc * S + fc * 512: dc * S + (fc + 1) * 512],
                                          start=(dc == 0), stop=(dc == NC - 1))
                        sm.inc(mi, "s_po", 1)
        esC.close()

    if "B" in phases or "C" in phases:
        esBC.close()

    if "D" not in phases and "B" not in phases and "C" not in phases:
        with nc.Block(no_gpsimd_drain=True) as blk:
            @blk.sync
            def _(sync):
                for i in range(NT):
                    sm.inc(sync.dma_start(out_d[i * 128:(i + 1) * 128, :], x_d[i * 128:(i + 1) * 128, :]), "s_fin", 16)
                sync.wait_ge(sm.get("s_fin"), sm.val("s_fin"))

    return nc, dbg


def prep_inputs(x_full, Wq, Wk, Wv, Wo, wg, wl):
    def qkv_img(W):
        A = np.ascontiguousarray(W.reshape(H, 128, NC, 128).transpose(0, 3, 2, 1))
        return A.reshape(H * 128, DM)
    wq_i = qkv_img(Wq); wk_i = qkv_img(Wk); wv_i = qkv_img(Wv)
    wo_i = np.ascontiguousarray(Wo.T).reshape(NC * 128, DM)
    def wg_img(W):
        A = np.ascontiguousarray(W.reshape(NC, 128, 64, 128).transpose(2, 1, 0, 3))
        return A.reshape(64 * 128, DM)
    wg0_i = wg_img(wg[0]); wg1_i = wg_img(wg[1])
    wl_i = np.ascontiguousarray(wl)
    idf = np.eye(128, dtype=np.float32)
    idb = np.eye(128).astype(ml_dtypes.bfloat16)
    mvals = np.where(np.arange(128)[:, None] > np.arange(128)[None, :], -1e9, 0.0)
    mskv = mvals.astype(ml_dtypes.bfloat16)
    epsv = np.full((128, 1), 1e-5, np.float32)
    shared = dict(wq_d=wq_i, wk_d=wk_i, wv_d=wv_i, wo_d=wo_i,
                  wg0_d=wg0_i, wg1_d=wg1_i, wl_d=wl_i,
                  idf_d=idf, idb_d=idb, msk_d=mskv, eps_d=epsv)
    maps = []
    for b in range(x_full.shape[0]):
        m = dict(shared)
        m["x_d"] = np.ascontiguousarray(x_full[b])
        maps.append(m)
    return maps





from concourse.bass_utils import run_bass_kernel_spmd

_NC_CACHE = {}

def _get_nc():
    if "nc" not in _NC_CACHE:
        _NC_CACHE["nc"] = build_nc(phases=("A", "D", "B", "C"))[0]
    return _NC_CACHE["nc"]


def kernel(L0_attn_x, L0_attn_Wq, L0_attn_Wk, L0_attn_Wv, L0_attn_Wo,
           L0_ffn_w_gating, L0_ffn_w_linear):
    x = np.asarray(L0_attn_x, dtype=np.float32)
    nc = _get_nc()
    maps = prep_inputs(x,
                       np.asarray(L0_attn_Wq, dtype=np.float32),
                       np.asarray(L0_attn_Wk, dtype=np.float32),
                       np.asarray(L0_attn_Wv, dtype=np.float32),
                       np.asarray(L0_attn_Wo, dtype=np.float32),
                       np.asarray(L0_ffn_w_gating, dtype=np.float32),
                       np.asarray(L0_ffn_w_linear, dtype=np.float32))
    res = run_bass_kernel_spmd(nc, maps, core_ids=list(range(8)))
    return np.stack([np.asarray(res.results[b]["out_d"], dtype=np.float32) for b in range(8)], axis=0)


# version 17
# speedup vs baseline: 1.0564x; 1.0564x over previous
import numpy as np
import ml_dtypes
from contextlib import ExitStack
import concourse.bass as bass
import concourse.mybir as mybir

F32 = mybir.dt.float32
BF16 = mybir.dt.bfloat16
FP8 = mybir.dt.float8e4
AF = mybir.ActivationFunctionType
ALU = mybir.AluOpType
AXL = mybir.AxisListType
DR = mybir.MatmulPerfMode.DoubleRow

S = 2048
DM = 2048
H = 16
NT = 16
NC = 16
NQ = 4
QH = 16
INVSQRT_HD = 0.08838834764831845
C1 = 0.796875
C2 = 0.0446777344


class Sems:
    def __init__(self, nc, es):
        self.nc = nc; self.es = es; self.h = {}; self.n = {}
    def get(self, name):
        if name not in self.h:
            self.h[name] = self.es.enter_context(self.nc.semaphore(name))
            self.n[name] = 0
        return self.h[name]
    def inc(self, instr, name, k=1):
        instr.then_inc(self.get(name), k)
        self.n[name] += k
        return self.n[name]
    def val(self, name):
        self.get(name)
        return self.n[name]


def score_tiles():
    out = []
    i = 0
    for Bi in range(4):
        for J in range(4 * Bi + 4):
            out.append((i, Bi, J, J * 128 - Bi * 512))
            i += 1
    return out


def build_nc(phases=("A", "D", "B", "C"), debug=(), d_skip=()):
    nc = bass.Bass("TRN2", target_bir_lowering=False, detect_race_conditions=False)
    es = ExitStack()
    sm = Sems(nc, es)

    x_d   = nc.dram_tensor("x_d",   [S, DM], F32, kind="ExternalInput")
    wq_d  = nc.dram_tensor("wq_d",  [H * 128, DM], F32, kind="ExternalInput")
    wk_d  = nc.dram_tensor("wk_d",  [H * 128, DM], F32, kind="ExternalInput")
    wv_d  = nc.dram_tensor("wv_d",  [H * 128, DM], F32, kind="ExternalInput")
    wo_d  = nc.dram_tensor("wo_d",  [NC * 128, DM], F32, kind="ExternalInput")
    wg0_d = nc.dram_tensor("wg0_d", [64 * 128, DM], F32, kind="ExternalInput")
    wg1_d = nc.dram_tensor("wg1_d", [64 * 128, DM], F32, kind="ExternalInput")
    wl_d  = nc.dram_tensor("wl_d",  [64 * 128, DM], F32, kind="ExternalInput")
    idf_d = nc.dram_tensor("idf_d", [128, 128], F32, kind="ExternalInput")
    idb_d = nc.dram_tensor("idb_d", [128, 128], BF16, kind="ExternalInput")
    msk_d = nc.dram_tensor("msk_d", [128, 128], BF16, kind="ExternalInput")
    eps_d = nc.dram_tensor("eps_d", [128, 1], F32, kind="ExternalInput")

    out_d = nc.dram_tensor("out_d", [S, DM], F32, kind="ExternalOutput")
    xn_h  = nc.dram_tensor("xn_h",  [NC * 128, S], BF16)
    y_h   = nc.dram_tensor("y_h",   [S, DM], F32)

    dbg = {}
    if "xnqx" in debug:
        dbg["xn_o"] = nc.dram_tensor("xn_o", [NC * 128, S], BF16, kind="ExternalOutput")
        dbg["qx_o"] = nc.dram_tensor("qx_o", [NC * 128, S], FP8, kind="ExternalOutput")
    if "y" in debug:
        dbg["y_o"] = nc.dram_tensor("y_o", [S, DM], F32, kind="ExternalOutput")
    if "stats" in debug:
        dbg["st_o"] = nc.dram_tensor("st_o", [128, 4 * NT], F32, kind="ExternalOutput")
    if "c" in debug:
        dbg["c_o"] = nc.dram_tensor("c_o", [128, NT * DM], BF16, kind="ExternalOutput")
        dbg["r_o"] = nc.dram_tensor("r_o", [128, H * NT], F32, kind="ExternalOutput")

    def mk_sb(stack):
        def f(name, shape, dt):
            return stack.enter_context(nc.sbuf_tensor(name, shape, dt))
        return f

    SBg = mk_sb(es)
    idf    = SBg("idf", [128, 128], F32)
    idb    = SBg("idb", [128, 128], BF16)
    msk    = SBg("msk", [128, 128], BF16)
    eps_ap = SBg("eps_ap", [128, 1], F32)

    with nc.Block(no_gpsimd_drain=True) as blk:
        @blk.sync
        def _(sync):
            sm.inc(sync.dma_start(idf[:, :], idf_d[:, :]), "misc", 16)
            sm.inc(sync.dma_start(idb[:, :], idb_d[:, :]), "misc", 16)
            sm.inc(sync.dma_start(msk[:, :], msk_d[:, :]), "misc", 16)
            sm.inc(sync.dma_start(eps_ap[:, :], eps_d[:, :]), "misc", 16)
            sync.wait_ge(sm.get("misc"), sm.val("misc"))

    esAD = ExitStack()
    SAD = mk_sb(esAD)
    qx_sb = SAD("qx_sb", [128, NC * S], FP8)
    wg0_t = [SAD(f"wg0_t{r}", [128, DM], FP8) for r in range(2)]
    wg1_t = [SAD(f"wg1_t{r}", [128, DM], FP8) for r in range(2)]

    if "A" in phases:
        esA = ExitStack()
        SA = mk_sb(esA)
        xbuf   = [SA(f"xbuf{i}", [128, DM], F32) for i in range(8)]
        xn_sbA = SA("xn_sbA", [128, NC * S], BF16)
        xntok  = [SA(f"xntok{r}", [128, DM], F32) for r in range(2)]
        sqscr  = SA("sqscr", [128, DM], F32)
        sumx   = SA("sumx", [128, NT], F32)
        sumsq  = SA("sumsq", [128, NT], F32)
        mm_    = SA("mm_", [128, NT], F32)
        vv_    = SA("vv_", [128, NT], F32)
        msq_   = SA("msq_", [128, NT], F32)
        vvm_   = SA("vvm_", [128, NT], F32)
        sd_    = SA("sd_", [128, NT], F32)
        rs_    = SA("rs_", [128, NT], F32)
        PT = [esA.enter_context(nc.psum_tensor(f"PTa{r}", [128, 128], F32)) for r in range(4)]

        with nc.Block(no_gpsimd_drain=True) as blk:
            @blk.sync
            def _(sync):
                for t in range(0, NT, 2):
                    i = t % 8
                    if t >= 8:
                        sync.wait_ge(sm.get("s_n"), t - 8 + 1)
                    sm.inc(sync.dma_start(xbuf[i][:, :], x_d[t * 128:(t + 1) * 128, :]), f"s_xb{i}", 16)
                sync.wait_ge(sm.get("s_xnb"), 256)
                if "stats" in debug:
                    for (bi, buf) in enumerate((sumx, sumsq, mm_, rs_)):
                        sm.inc(sync.dma_start(dbg["st_o"][:, bi * NT:(bi + 1) * NT], buf[:, :]), "s_dbgs", 16)
                for c in range(NC):
                    sm.inc(sync.dma_start(xn_h[c * 128:(c + 1) * 128, :], xn_sbA[:, c * S:(c + 1) * S]), "s_xn_out", 16)
                if "xnqx" in debug:
                    for c in range(NC):
                        sm.inc(sync.dma_start(dbg["xn_o"][c * 128:(c + 1) * 128, :], xn_sbA[:, c * S:(c + 1) * S]), "s_dbg", 16)
                    sync.wait_ge(sm.get("s_qx"), 256)
                    for c in range(NC):
                        sm.inc(sync.dma_start(dbg["qx_o"][c * 128:(c + 1) * 128, :], qx_sb[:, c * S:(c + 1) * S]), "s_dbg", 16)
                    sync.wait_ge(sm.get("s_dbg"), sm.val("s_dbg"))

            @blk.gpsimd
            def _(gpsimd):
                if "D" in phases:
                    for ht in range(2):
                        sm.inc(gpsimd.dma_start(wg0_t[ht % 2][:, :], wg0_d[ht * 128:(ht + 1) * 128, :]), f"s_wg0_{ht % 2}", 16)
                        sm.inc(gpsimd.dma_start(wg1_t[ht % 2][:, :], wg1_d[ht * 128:(ht + 1) * 128, :]), f"s_wg1_{ht % 2}", 16)

            @blk.vector
            def _(vector):
                def xn_evac(j):
                    for c in range(NC):
                        vector.wait_ge(sm.get("s_ta"), j * 16 + c + 1)
                        sm.inc(vector.tensor_copy(xn_sbA[:, c * S + j * 128: c * S + (j + 1) * 128],
                                                  PT[(j * 16 + c) % 4][:, :]), "s_xnb", 1)
                def norm(t, k):
                    if t >= 2:
                        vector.wait_ge(sm.get("s_ta"), (t - 1) * 16)
                    sm.inc(vector.tensor_scalar(xntok[t % 2][:, :], xbuf[k][:, :],
                                                mm_[:, t:t + 1], rs_[:, t:t + 1],
                                                op0=ALU.subtract, op1=ALU.mult), "s_n", 1)
                for b in range(2):
                    for k in range(8):
                        t = b * 8 + k
                        vector.wait_ge(sm.get(f"s_xb{k}"), 16 * (b + 1))
                        sm.inc(vector.reduce_sum(sumx[:, t:t + 1], xbuf[k][:, :], axis=AXL.X), "s_v1", 1)
                    cols = slice(b * 8, b * 8 + 8)
                    vector.wait_ge(sm.get("s_s1"), 8 * (b + 1))
                    vector.wait_ge(sm.get("s_v1"), 8 * (b + 1))
                    sm.inc(vector.tensor_scalar_mul(mm_[:, cols], sumx[:, cols], 1.0 / DM), "s_v2", 1)
                    sm.inc(vector.tensor_scalar_mul(vv_[:, cols], sumsq[:, cols], 1.0 / DM), "s_v5", 1)
                    vector.wait_ge(sm.get("s_s2"), b + 1)
                    vector.wait_ge(sm.get("s_v5"), b + 1)
                    sm.inc(vector.tensor_sub(vvm_[:, cols], vv_[:, cols], msq_[:, cols]), "s_v3", 1)
                    vector.wait_ge(sm.get("s_s3"), b + 1)
                    sm.inc(vector.reciprocal(rs_[:, cols], sd_[:, cols]), "s_v4", 1)
                    vector.wait_ge(sm.get("s_v4"), b + 1)
                    for k in range(8):
                        t = b * 8 + k
                        if t >= 1:
                            xn_evac(t - 1)
                        norm(t, k)
                xn_evac(NT - 1)

            @blk.scalar
            def _(scalar):
                def qx_evac(j):
                    for c in range(NC):
                        scalar.wait_ge(sm.get("s_ta"), j * 16 + c + 1)
                        scalar.wait_ge(sm.get("s_xnb"), j * 16 + c + 1)
                        sm.inc(scalar.copy(qx_sb[:, c * S + j * 128: c * S + (j + 1) * 128],
                                           PT[(j * 16 + c) % 4][:, :]), "s_qx", 1)
                def stats(b):
                    for k in range(8):
                        t = b * 8 + k
                        scalar.wait_ge(sm.get(f"s_xb{k}"), 16 * (b + 1))
                        sm.inc(scalar.activation(sqscr[:, :], xbuf[k][:, :], AF.Square,
                                                 accum_out=sumsq[:, t:t + 1]), "s_s1", 1)
                    cols = slice(b * 8, b * 8 + 8)
                    scalar.wait_ge(sm.get("s_v2"), b + 1)
                    sm.inc(scalar.activation(msq_[:, cols], mm_[:, cols], AF.Square), "s_s2", 1)
                    scalar.wait_ge(sm.get("s_v3"), b + 1)
                    sm.inc(scalar.activation(sd_[:, cols], vvm_[:, cols], AF.Sqrt, bias=eps_ap[:, :]), "s_s3", 1)
                for t in range(1, 8, 2):
                    sm.inc(scalar.dma_start(xbuf[t][:, :], x_d[t * 128:(t + 1) * 128, :]), f"s_xb{t}", 16)
                stats(0)
                for j in range(0, 7):
                    qx_evac(j)
                for t in range(9, NT, 2):
                    scalar.wait_ge(sm.get("s_n"), t - 8 + 1)
                    sm.inc(scalar.dma_start(xbuf[t % 8][:, :], x_d[t * 128:(t + 1) * 128, :]), f"s_xb{t % 8}", 16)
                stats(1)
                for j in range(7, NT):
                    qx_evac(j)

            @blk.tensor
            def _(tensor):
                tensor.wait_ge(sm.get("misc"), 64)
                for t in range(NT):
                    tensor.wait_ge(sm.get("s_n"), t + 1)
                    for c in range(NC):
                        k = t * 16 + c
                        if k >= 4:
                            tensor.wait_ge(sm.get("s_xnb"), k - 3)
                            tensor.wait_ge(sm.get("s_qx"), k - 3)
                        sm.inc(tensor.transpose(PT[k % 4][:, :], xntok[t % 2][:, c * 128:(c + 1) * 128],
                                                idf[:, :]), "s_ta", 1)
        esA.close()

    if "D" in phases:
        esD = ExitStack()
        SD = mk_sb(esD)
        act_sb = SD("act_sb", [128, QH * S], FP8)
        wl_sb  = SD("wl_sb", [128, QH * S], FP8)
        g_sb   = SD("g_sb", [128, DM], BF16)
        f1_sb  = SD("f1_sb", [128, DM], BF16)
        e1     = SD("e1", [128, DM], BF16)
        e2     = SD("e2", [128, DM], BF16)
        gel    = SD("gel", [128, DM], BF16)
        xy     = [SD(f"xy{r}", [128, DM], F32) for r in range(2)]
        yo     = [SD(f"yo{r}", [128, DM], F32) for r in range(2)]
        PA = esD.enter_context(nc.psum_tensor("PA", [128, DM], F32))
        PB = esD.enter_context(nc.psum_tensor("PB", [128, DM], F32))

        with nc.Block(no_gpsimd_drain=True) as blk:
            @blk.sync
            def _(sync):
                if "A" in phases:
                    sync.wait_ge(sm.get("s_xn_out"), 256)
                for q in range(NQ):
                    for i in range(NT):
                        if "d2" in d_skip:
                            break
                        n = q * NT + i
                        if n >= 2:
                            sync.wait_ge(sm.get("s_ya"), 4 * (n - 1))
                        src = x_d if q == 0 else y_h
                        sm.inc(sync.dma_start(xy[i % 2][:, :], src[i * 128:(i + 1) * 128, :]), f"s_xy{n % 2}", 16)
                        sync.wait_ge(sm.get("s_ya"), 4 * (n + 1))
                        sm.inc(sync.dma_start(y_h[i * 128:(i + 1) * 128, :], yo[i % 2][:, :]), f"s_yw{n % 2}", 16)
                if "y" in debug and "d2" not in d_skip:
                    sync.wait_ge(sm.get("s_yw0"), 16 * NQ * NT // 2)
                    sync.wait_ge(sm.get("s_yw1"), 16 * NQ * NT // 2)
                    for i in range(NT):
                        sm.inc(sync.dma_start(dbg["y_o"][i * 128:(i + 1) * 128, :], y_h[i * 128:(i + 1) * 128, :]), "s_dbg2", 16)
                    sync.wait_ge(sm.get("s_dbg2"), sm.val("s_dbg2"))
                if "C" not in phases:
                    if "d2" not in d_skip:
                        sync.wait_ge(sm.get("s_yw0"), 16 * NQ * NT // 2)
                        sync.wait_ge(sm.get("s_yw1"), 16 * NQ * NT // 2)
                    for i in range(NT):
                        sm.inc(sync.dma_start(out_d[i * 128:(i + 1) * 128, :], y_h[i * 128:(i + 1) * 128, :]), "s_fin", 16)
                    sync.wait_ge(sm.get("s_fin"), sm.val("s_fin"))

            @blk.gpsimd
            def _(gpsimd):
                if "A" in phases:
                    gpsimd.wait_ge(sm.get("s_xn_out"), 256)
                pre = 2 if "A" in phases else 0
                def wg_load(n):
                    if n >= pre and n < NQ * QH:
                        if n >= 2:
                            gpsimd.wait_ge(sm.get("s_gf"), 2 * (n - 1))
                        sm.inc(gpsimd.dma_start(wg0_t[n % 2][:, :], wg0_d[n * 128:(n + 1) * 128, :]), f"s_wg0_{n % 2}", 16)
                        sm.inc(gpsimd.dma_start(wg1_t[n % 2][:, :], wg1_d[n * 128:(n + 1) * 128, :]), f"s_wg1_{n % 2}", 16)
                for q in range(NQ):
                    for ht in range(QH):
                        wg_load(q * QH + ht)
                        if "d2" not in d_skip:
                            row = q * QH + ht
                            if q >= 1:
                                gpsimd.wait_ge(sm.get("s_d2"), 4 * NT * q)
                            sm.inc(gpsimd.dma_start(wl_sb[:, ht * S:(ht + 1) * S],
                                                    wl_d[row * 128:(row + 1) * 128, :]), f"s_wl{q % 2}", 16)

            @blk.tensor
            def _(tensor):
                if "A" in phases:
                    tensor.wait_ge(sm.get("s_qx"), 256)
                for q in range(NQ):
                    if q >= 1:
                        tensor.wait_ge(sm.get("s_ya"), 64 * q - 4)
                    for ht in range(QH):
                        n = q * QH + ht
                        tensor.wait_ge(sm.get(f"s_wg0_{n % 2}"), 16 * (n // 2 + 1))
                        tensor.wait_ge(sm.get(f"s_wg1_{n % 2}"), 16 * (n // 2 + 1))
                        if n >= 1:
                            tensor.wait_ge(sm.get("s_gev"), 2 * n - 1)
                        for tc in range(4):
                            for c2 in range(NC // 2):
                                mi = tensor.matmul(PA[:, tc * 512:(tc + 1) * 512],
                                              bass.AP(wg0_t[n % 2], 2 * c2 * 128, [[S, 128], [128, 2], [1, 128]]),
                                              bass.AP(qx_sb, 2 * c2 * S + tc * 512, [[NC * S, 128], [S, 2], [1, 512]]),
                                              start=(c2 == 0), stop=(c2 == NC // 2 - 1), perf_mode=DR)
                        sm.inc(mi, "s_gf", 1)
                        if n >= 1:
                            tensor.wait_ge(sm.get("s_gev"), 2 * n)
                        if q >= 1 and ht == 0:
                            tensor.wait_ge(sm.get("s_ya"), 64 * q)
                        for tc in range(4):
                            for c2 in range(NC // 2):
                                mi = tensor.matmul(PB[:, tc * 512:(tc + 1) * 512],
                                              bass.AP(wg1_t[n % 2], 2 * c2 * 128, [[S, 128], [128, 2], [1, 128]]),
                                              bass.AP(qx_sb, 2 * c2 * S + tc * 512, [[NC * S, 128], [S, 2], [1, 512]]),
                                              start=(c2 == 0), stop=(c2 == NC // 2 - 1), perf_mode=DR)
                        sm.inc(mi, "s_gf", 1)
                    if "d2" in d_skip:
                        continue
                    tensor.wait_ge(sm.get("s_act"), (q + 1) * QH)
                    tensor.wait_ge(sm.get(f"s_wl{q % 2}"), 16 * QH * (q // 2 + 1))
                    tensor.wait_ge(sm.get("s_gev"), 2 * QH * (q + 1))
                    for i in range(NT):
                        n2 = q * NT + i
                        P = PA if i % 2 == 0 else PB
                        if n2 >= 2:
                            tensor.wait_ge(sm.get("s_ya"), 4 * (n2 - 1))
                        for fc in range(4):
                            for h2 in range(QH // 2):
                                mi = tensor.matmul(P[:, fc * 512:(fc + 1) * 512],
                                              bass.AP(act_sb, 2 * h2 * S + i * 128, [[QH * S, 128], [S, 2], [1, 128]]),
                                              bass.AP(wl_sb, 2 * h2 * S + fc * 512, [[QH * S, 128], [S, 2], [1, 512]]),
                                              start=(h2 == 0), stop=(h2 == QH // 2 - 1), perf_mode=DR)
                            sm.inc(mi, "s_d2", 1)

            @blk.vector
            def _(vector):
                for q in range(NQ):
                    if q >= 1 and "d2" not in d_skip:
                        vector.wait_ge(sm.get("s_d2"), 4 * NT * q)
                    for ht in range(QH):
                        n = q * QH + ht
                        vector.wait_ge(sm.get("s_gev"), 2 * n + 1)
                        vector.tensor_mul(e1[:, :], g_sb[:, :], g_sb[:, :])
                        vector.tensor_mul(e2[:, :], e1[:, :], g_sb[:, :])
                        vector.tensor_scalar_mul(e1[:, :], e2[:, :], C2)
                        vector.tensor_add(e2[:, :], g_sb[:, :], e1[:, :])
                        sm.inc(vector.tensor_scalar_mul(e1[:, :], e2[:, :], C1), "s_w", 1)
                        vector.wait_ge(sm.get("s_th"), n + 1)
                        vector.wait_ge(sm.get("s_gev"), 2 * n + 2)
                        vector.tensor_scalar(e2[:, :], gel[:, :], 1.0, 0.5, op0=ALU.add, op1=ALU.mult)
                        vector.tensor_mul(e1[:, :], g_sb[:, :], e2[:, :])
                        vector.tensor_mul(e2[:, :], e1[:, :], f1_sb[:, :])
                        sm.inc(vector.tensor_copy(act_sb[:, ht * S:(ht + 1) * S], e2[:, :]), "s_act", 1)
                    for i in range(NT):
                        if "d2" in d_skip:
                            break
                        n2 = q * NT + i
                        P = PA if i % 2 == 0 else PB
                        vector.wait_ge(sm.get(f"s_xy{n2 % 2}"), 16 * (n2 // 2 + 1))
                        if n2 >= 2:
                            vector.wait_ge(sm.get(f"s_yw{n2 % 2}"), 16 * (n2 // 2))
                        for fc in range(4):
                            vector.wait_ge(sm.get("s_d2"), 4 * n2 + fc + 1)
                            sm.inc(vector.tensor_add(yo[i % 2][:, fc * 512:(fc + 1) * 512],
                                                     P[:, fc * 512:(fc + 1) * 512],
                                                     xy[i % 2][:, fc * 512:(fc + 1) * 512]), "s_ya", 1)

            @blk.scalar
            def _(scalar):
                for q in range(NQ):
                    for ht in range(QH):
                        n = q * QH + ht
                        scalar.wait_ge(sm.get("s_gf"), 2 * n + 1)
                        if n >= 1:
                            scalar.wait_ge(sm.get("s_act"), n)
                        sm.inc(scalar.copy(g_sb[:, :], PA[:, :]), "s_gev", 1)
                        scalar.wait_ge(sm.get("s_gf"), 2 * n + 2)
                        sm.inc(scalar.copy(f1_sb[:, :], PB[:, :]), "s_gev", 1)
                        scalar.wait_ge(sm.get("s_w"), n + 1)
                        sm.inc(scalar.activation(gel[:, :], e1[:, :], AF.Tanh), "s_th", 1)
        esD.close()
    esAD.close()

    if "B" in phases or "C" in phases:
        esBC = ExitStack()
        SBC = mk_sb(esBC)
        c_sb = SBC("c_sb", [128, NT * DM], BF16)
        r_sb = SBC("r_sb", [128, H * NT], F32)

    if "B" in phases:
        esB = ExitStack()
        SBB = mk_sb(esB)
        xn_sbB = SBB("xn_sbB", [128, NC * S], BF16)
        q_sbh  = [SBB(f"q_sbh{r}", [128, S], BF16) for r in range(2)]
        k_sbh  = [SBB(f"k_sbh{r}", [128, S], BF16) for r in range(2)]
        vt_sbh = [SBB(f"vt_sbh{r}", [128, S], BF16) for r in range(2)]
        v_sbh  = [SBB(f"v_sbh{r}", [128, NT * 192], BF16) for r in range(2)]
        pt_sb  = SBB("pt_sb", [128, NT * 512], BF16)
        wqt = [SBB(f"wqt{r}", [128, S], BF16) for r in range(2)]
        wkt = [SBB(f"wkt{r}", [128, S], BF16) for r in range(2)]
        wvt = [SBB(f"wvt{r}", [128, S], BF16) for r in range(2)]
        lnS = SBB("lnS", [128, NT], F32)
        PP  = [esB.enter_context(nc.psum_tensor(f"PPb{r}", [128, 512], F32)) for r in range(2)]
        PS_ = [esB.enter_context(nc.psum_tensor(f"PSb{r}", [128, 1024], F32)) for r in range(2)]
        PC_ = [esB.enter_context(nc.psum_tensor(f"PCb{r}", [128, 132], F32)) for r in range(2)]
        

        ST = score_tiles()
        NSC = len(ST)
        CPH = 16

        with nc.Block(no_gpsimd_drain=True) as blk:
            @blk.sync
            def _(sync):
                for c in range(NC):
                    sm.inc(sync.dma_start(xn_sbB[:, c * S:(c + 1) * S], xn_h[c * 128:(c + 1) * 128, :]), f"s_xni{c // 4}", 16)
                for h in range(H):
                    sync.wait_ge(sm.get("s_pe"), 12 * (h + 1))
                    if h >= 2:
                        sync.wait_ge(sm.get("s_cm"), (h - 1) * CPH)
                    for jt in range(NT):
                        sm.inc(sync.dma_start_transpose(v_sbh[h % 2][:, jt * 192: jt * 192 + 128],
                                                        vt_sbh[h % 2][:, jt * 128:(jt + 1) * 128]), f"s_vtt{h % 2}", 16)
                if "c" in debug:
                    sync.wait_ge(sm.get("s_r"), H)
                    sync.wait_ge(sm.get("s_cev"), 2 * H * NT)
                    sm.inc(sync.dma_start(dbg["c_o"][:, :], c_sb[:, :]), "s_dbg3", 16)
                    sm.inc(sync.dma_start(dbg["r_o"][:, :], r_sb[:, :]), "s_dbg3", 16)
                    sync.wait_ge(sm.get("s_dbg3"), sm.val("s_dbg3"))
                if "C" not in phases:
                    sync.wait_ge(sm.get("s_r"), H)
                    for i in range(NT):
                        sm.inc(sync.dma_start(out_d[i * 128:(i + 1) * 128, :], y_h[i * 128:(i + 1) * 128, :]), "s_fin", 16)
                    sync.wait_ge(sm.get("s_fin"), sm.val("s_fin"))

            @blk.gpsimd
            def _(gpsimd):
                for h in range(H):
                    if h >= 2:
                        gpsimd.wait_ge(sm.get("s_pj"), 12 * (h - 1))
                    sm.inc(gpsimd.dma_start(wqt[h % 2][:, :], wq_d[h * 128:(h + 1) * 128, :]), f"s_wq{h % 2}", 16)
                    sm.inc(gpsimd.dma_start(wkt[h % 2][:, :], wk_d[h * 128:(h + 1) * 128, :]), f"s_wk{h % 2}", 16)
                    sm.inc(gpsimd.dma_start(wvt[h % 2][:, :], wv_d[h * 128:(h + 1) * 128, :]), f"s_wv{h % 2}", 16)

            @blk.tensor
            def _(tensor):
                tensor.wait_ge(sm.get("misc"), 64)
                tensor.wait_ge(sm.get("s_vone"), 2)
                nsc = 0; ncc = 0; npj = [0]

                def proj_chain(hh, k):
                    pi, tc = divmod(k, 4)
                    wt = (wqt, wkt, wvt)[pi]
                    if k == 0:
                        tensor.wait_ge(sm.get(f"s_wq{hh % 2}"), 16 * (hh // 2 + 1))
                        tensor.wait_ge(sm.get(f"s_wk{hh % 2}"), 16 * (hh // 2 + 1))
                        tensor.wait_ge(sm.get(f"s_wv{hh % 2}"), 16 * (hh // 2 + 1))
                    if npj[0] >= 2:
                        tensor.wait_ge(sm.get("s_pe"), npj[0] - 1)
                    for c in range(NC):
                        if npj[0] == 0:
                            tensor.wait_ge(sm.get(f"s_xni{c // 4}"), 64)
                        mi = tensor.matmul(PP[npj[0] % 2][:, :],
                                      wt[hh % 2][:, c * 128:(c + 1) * 128],
                                      xn_sbB[:, c * S + tc * 512: c * S + (tc + 1) * 512],
                                      start=(c == 0), stop=(c == NC - 1))
                    sm.inc(mi, "s_pj", 1)
                    npj[0] += 1

                for k in range(12):
                    proj_chain(0, k)
                npr = 0
                for h in range(H):
                    for Bi in range(4):
                        Jmax = 4 * Bi + 4
                        kpos = {max(1, Jmax * (k + 1) // 4): 3 * Bi + k for k in range(3)}
                        for J in range(Jmax):
                            o = J * 128 - Bi * 512
                            if J % 2 == 0 and npr >= 2:
                                tensor.wait_ge(sm.get("s_exp"), npr - 1)
                            mi = tensor.matmul(PS_[npr % 2][:, (J % 2) * 512: (J % 2) * 512 + 512],
                                          k_sbh[h % 2][:, J * 128:(J + 1) * 128],
                                          q_sbh[h % 2][:, Bi * 512: Bi * 512 + 512],
                                          start=True, stop=True)
                            sm.inc(mi, "s_sc", 1)
                            nsc += 1
                            if J % 2 == 1:
                                npr += 1
                            if h + 1 < H and (J + 1) in kpos:
                                proj_chain(h + 1, kpos[J + 1])
                        tensor.wait_ge(sm.get("s_exp"), npr)
                        tensor.wait_ge(sm.get(f"s_vtt{h % 2}"), 256 * (h // 2 + 1))
                        for t4 in range(4):
                            it = 4 * Bi + t4
                            if ncc >= 2:
                                tensor.wait_ge(sm.get("s_cev"), 2 * (ncc - 1))
                            for J in range(it + 1):
                                mi = tensor.matmul(PC_[ncc % 2][:, 0:129],
                                              pt_sb[:, J * 512 + t4 * 128: J * 512 + (t4 + 1) * 128],
                                              v_sbh[h % 2][:, J * 192: J * 192 + 129],
                                              start=(J == 0), stop=(J == it))
                            sm.inc(mi, "s_cm", 1)
                            ncc += 1

            @blk.vector
            def _(vector):
                for r in range(2):
                    for jt in range(NT):
                        mi = vector.memset(v_sbh[r][:, jt * 192 + 128: jt * 192 + 129], 1.0)
                    sm.inc(mi, "s_vone", 1)
                npj = 0; nvt = 0

                def proj_evac(hh, k):
                    nonlocal npj
                    pi = k // 4
                    dst = (q_sbh, k_sbh, vt_sbh)[pi][hh % 2]
                    if hh >= 2:
                        if k == 0:
                            vector.wait_ge(sm.get("s_sc"), (hh - 1) * NSC)
                        if k == 8:
                            vector.wait_ge(sm.get(f"s_vtt{hh % 2}"), 256 * (hh // 2))
                    vector.wait_ge(sm.get("s_pj"), npj + 1)
                    sm.inc(vector.tensor_copy(dst[:, (k % 4) * 512:(k % 4 + 1) * 512], PP[npj % 2][:, :]), "s_pe", 1)
                    npj += 1

                for k in range(12):
                    proj_evac(0, k)
                nprv = 0
                for h in range(H):
                    for Bi in range(4):
                        Jmax = 4 * Bi + 4
                        kpos = {max(1, Jmax * (k + 1) // 4): 3 * Bi + k for k in range(3)}
                        base = sum(4 * b + 4 for b in range(Bi))
                        for J in range(Jmax):
                            o = J * 128 - Bi * 512
                            if o >= 0:
                                g = h * NSC + base + J
                                vector.wait_ge(sm.get("s_sc"), g + 1)
                                sm.inc(vector.tensor_tensor(PS_[nprv % 2][:, (J % 2) * 512 + o: (J % 2) * 512 + o + 128],
                                                            PS_[nprv % 2][:, (J % 2) * 512 + o: (J % 2) * 512 + o + 128],
                                                            msk[:, :], op=ALU.add), "s_mk", 1)
                            if J % 2 == 1:
                                nprv += 1
                            if h + 1 < H and (J + 1) in kpos:
                                proj_evac(h + 1, kpos[J + 1])

            @blk.scalar
            def _(scalar):
                nmk = 0; nprs = 0
                for h in range(H):
                    for Bi in range(4):
                        if h > 0 or Bi > 0:
                            scalar.wait_ge(sm.get("s_cm"), h * CPH + 4 * Bi)
                        for Jl in range(0, 4 * Bi + 4, 2):
                            zero_after = []
                            for J in (Jl, Jl + 1):
                                o = J * 128 - Bi * 512
                                idx = sum(4 * b + 4 for b in range(Bi)) + J
                                g = h * NSC + idx
                                scalar.wait_ge(sm.get("s_sc"), g + 1)
                                if o >= 0:
                                    nmk += 1
                                    scalar.wait_ge(sm.get("s_mk"), nmk)
                                    if o > 0:
                                        zero_after.append((J, o))
                            sm.inc(scalar.activation(pt_sb[:, Jl * 512: (Jl + 2) * 512],
                                                     PS_[nprs % 2][:, 0:1024], AF.Exp,
                                                     scale=INVSQRT_HD), "s_exp", 1)
                            nprs += 1
                            for (J, o) in zero_after:
                                scalar.memzero(pt_sb[:, J * 512: J * 512 + o])
                            if zero_after:
                                scalar.wait_ge(sm.get("s_exp"), nprs)
                        for t4 in range(4):
                            it = 4 * Bi + t4
                            cidx = h * CPH + it
                            scalar.wait_ge(sm.get("s_cm"), cidx + 1)
                            sm.inc(scalar.activation(lnS[:, it:it + 1], PC_[cidx % 2][:, 128:129], AF.Ln), "s_cev", 1)
                            sm.inc(scalar.copy(c_sb[:, it * DM + h * 128: it * DM + (h + 1) * 128],
                                               PC_[cidx % 2][:, 0:128]), "s_cev", 1)
                    scalar.wait_ge(sm.get("s_cev"), 2 * (h + 1) * NT)
                    sm.inc(scalar.activation(r_sb[:, h * NT:(h + 1) * NT], lnS[:, :], AF.Exp, scale=-1.0), "s_r", 1)
        esB.close()

    if "C" in phases:
        esC = ExitStack()
        SC = mk_sb(esC)
        cTt = [SC(f"cTt{r}", [128, DM], BF16) for r in range(2)]
        wo_sb = SC("wo_sb", [128, NC * S], BF16)
        ys = [SC(f"ys{r}", [128, DM], F32) for r in range(2)]
        os_ = [SC(f"os{r}", [128, DM], F32) for r in range(2)]
        PTC = [esC.enter_context(nc.psum_tensor(f"PTc{r}", [128, 128], BF16)) for r in range(2)]
        PO  = [esC.enter_context(nc.psum_tensor(f"POc{r}", [128, 512], F32)) for r in range(4)]

        with nc.Block(no_gpsimd_drain=True) as blk:
            @blk.sync
            def _(sync):
                for i in range(NT):
                    if i >= 2:
                        sync.wait_ge(sm.get("s_oa"), 4 * (i - 1))
                    sm.inc(sync.dma_start(ys[i % 2][:, :], y_h[i * 128:(i + 1) * 128, :]), f"s_yc{i % 2}", 16)
                sync.wait_ge(sm.get("s_out0"), 16 * NT // 2)
                sync.wait_ge(sm.get("s_out1"), 16 * NT // 2)

            @blk.scalar
            def _(scalar):
                for i in range(NT):
                    scalar.wait_ge(sm.get("s_oa"), 4 * (i + 1))
                    sm.inc(scalar.dma_start(out_d[i * 128:(i + 1) * 128, :], os_[i % 2][:, :]), f"s_out{i % 2}", 16)
                scalar.wait_ge(sm.get("s_out0"), 16 * NT // 2)
                scalar.wait_ge(sm.get("s_out1"), 16 * NT // 2)

            @blk.gpsimd
            def _(gpsimd):
                for dc in range(NC):
                    sm.inc(gpsimd.dma_start(wo_sb[:, dc * S:(dc + 1) * S], wo_d[dc * 128:(dc + 1) * 128, :]), "s_wo", 16)

            @blk.vector
            def _(vector):
                for i in range(NT):
                    for hh in range(H):
                        mi = vector.tensor_scalar_mul(c_sb[:, i * DM + hh * 128: i * DM + (hh + 1) * 128],
                                                      c_sb[:, i * DM + hh * 128: i * DM + (hh + 1) * 128],
                                                      r_sb[:, hh * NT + i: hh * NT + i + 1])
                    sm.inc(mi, "s_csc", 1)
                for i in range(NT):
                    if i >= 2:
                        vector.wait_ge(sm.get("s_po"), 4 * (i - 1))
                    for dc in range(NC):
                        ntc = i * NC + dc
                        vector.wait_ge(sm.get("s_tc"), ntc + 1)
                        sm.inc(vector.tensor_copy(cTt[i % 2][:, dc * 128:(dc + 1) * 128], PTC[ntc % 2][:, :]), "s_cte", 1)
                    for fc in range(4):
                        npo = 4 * i + fc
                        vector.wait_ge(sm.get("s_po"), npo + 1)
                        vector.wait_ge(sm.get(f"s_yc{i % 2}"), 16 * (i // 2 + 1))
                        if i >= 2:
                            vector.wait_ge(sm.get(f"s_out{i % 2}"), 16 * (i // 2))
                        sm.inc(vector.tensor_add(os_[i % 2][:, fc * 512:(fc + 1) * 512],
                                                 PO[npo % 4][:, :], ys[i % 2][:, fc * 512:(fc + 1) * 512]), "s_oa", 1)

            @blk.tensor
            def _(tensor):
                for i in range(NT):
                    tensor.wait_ge(sm.get("s_csc"), i + 1)
                    for dc in range(NC):
                        ntc = i * NC + dc
                        if ntc >= 2:
                            tensor.wait_ge(sm.get("s_cte"), ntc - 1)
                        sm.inc(tensor.transpose(PTC[ntc % 2][:, :],
                                                c_sb[:, i * DM + dc * 128: i * DM + (dc + 1) * 128],
                                                idb[:, :]), "s_tc", 1)
                    tensor.wait_ge(sm.get("s_cte"), NC * (i + 1))
                    if i == 0:
                        tensor.wait_ge(sm.get("s_wo"), 16 * NC)
                    for fc in range(4):
                        npo = 4 * i + fc
                        if npo >= 4:
                            tensor.wait_ge(sm.get("s_oa"), npo - 3)
                        for dc in range(NC):
                            mi = tensor.matmul(PO[npo % 4][:, :],
                                          cTt[i % 2][:, dc * 128:(dc + 1) * 128],
                                          wo_sb[:, dc * S + fc * 512: dc * S + (fc + 1) * 512],
                                          start=(dc == 0), stop=(dc == NC - 1))
                        sm.inc(mi, "s_po", 1)
        esC.close()

    if "B" in phases or "C" in phases:
        esBC.close()

    if "D" not in phases and "B" not in phases and "C" not in phases:
        with nc.Block(no_gpsimd_drain=True) as blk:
            @blk.sync
            def _(sync):
                for i in range(NT):
                    sm.inc(sync.dma_start(out_d[i * 128:(i + 1) * 128, :], x_d[i * 128:(i + 1) * 128, :]), "s_fin", 16)
                sync.wait_ge(sm.get("s_fin"), sm.val("s_fin"))

    return nc, dbg


def prep_inputs(x_full, Wq, Wk, Wv, Wo, wg, wl):
    def qkv_img(W):
        A = np.ascontiguousarray(W.reshape(H, 128, NC, 128).transpose(0, 3, 2, 1))
        return A.reshape(H * 128, DM)
    wq_i = qkv_img(Wq); wk_i = qkv_img(Wk); wv_i = qkv_img(Wv)
    wo_i = np.ascontiguousarray(Wo.T).reshape(NC * 128, DM)
    def wg_img(W):
        A = np.ascontiguousarray(W.reshape(NC, 128, 64, 128).transpose(2, 1, 0, 3))
        return A.reshape(64 * 128, DM)
    wg0_i = wg_img(wg[0]); wg1_i = wg_img(wg[1])
    wl_i = np.ascontiguousarray(wl)
    idf = np.eye(128, dtype=np.float32)
    idb = np.eye(128).astype(ml_dtypes.bfloat16)
    mvals = np.where(np.arange(128)[:, None] > np.arange(128)[None, :], -1e9, 0.0)
    mskv = mvals.astype(ml_dtypes.bfloat16)
    epsv = np.full((128, 1), 1e-5, np.float32)
    shared = dict(wq_d=wq_i, wk_d=wk_i, wv_d=wv_i, wo_d=wo_i,
                  wg0_d=wg0_i, wg1_d=wg1_i, wl_d=wl_i,
                  idf_d=idf, idb_d=idb, msk_d=mskv, eps_d=epsv)
    maps = []
    for b in range(x_full.shape[0]):
        m = dict(shared)
        m["x_d"] = np.ascontiguousarray(x_full[b])
        maps.append(m)
    return maps

from concourse.bass_utils import run_bass_kernel_spmd

_NC_CACHE = {}

def _get_nc():
    if "nc" not in _NC_CACHE:
        _NC_CACHE["nc"] = build_nc(phases=("A", "D", "B", "C"))[0]
    return _NC_CACHE["nc"]


def kernel(L0_attn_x, L0_attn_Wq, L0_attn_Wk, L0_attn_Wv, L0_attn_Wo,
           L0_ffn_w_gating, L0_ffn_w_linear):
    x = np.asarray(L0_attn_x, dtype=np.float32)
    nc = _get_nc()
    maps = prep_inputs(x,
                       np.asarray(L0_attn_Wq, dtype=np.float32),
                       np.asarray(L0_attn_Wk, dtype=np.float32),
                       np.asarray(L0_attn_Wv, dtype=np.float32),
                       np.asarray(L0_attn_Wo, dtype=np.float32),
                       np.asarray(L0_ffn_w_gating, dtype=np.float32),
                       np.asarray(L0_ffn_w_linear, dtype=np.float32))
    res = run_bass_kernel_spmd(nc, maps, core_ids=list(range(8)))
    return np.stack([np.asarray(res.results[b]["out_d"], dtype=np.float32) for b in range(8)], axis=0)


# version 18 (final)
# speedup vs baseline: 1.0624x; 1.0057x over previous
import numpy as np
import ml_dtypes
from contextlib import ExitStack
import concourse.bass as bass
import concourse.mybir as mybir

F32 = mybir.dt.float32
BF16 = mybir.dt.bfloat16
FP8 = mybir.dt.float8e4
AF = mybir.ActivationFunctionType
ALU = mybir.AluOpType
AXL = mybir.AxisListType
DR = mybir.MatmulPerfMode.DoubleRow

S = 2048
DM = 2048
H = 16
NT = 16
NC = 16
NQ = 4
QH = 16
INVSQRT_HD = 0.08838834764831845
C1 = 0.796875
C2 = 0.0446777344


class Sems:
    def __init__(self, nc, es):
        self.nc = nc; self.es = es; self.h = {}; self.n = {}
    def get(self, name):
        if name not in self.h:
            self.h[name] = self.es.enter_context(self.nc.semaphore(name))
            self.n[name] = 0
        return self.h[name]
    def inc(self, instr, name, k=1):
        instr.then_inc(self.get(name), k)
        self.n[name] += k
        return self.n[name]
    def val(self, name):
        self.get(name)
        return self.n[name]


def score_tiles():
    out = []
    i = 0
    for Bi in range(4):
        for J in range(4 * Bi + 4):
            out.append((i, Bi, J, J * 128 - Bi * 512))
            i += 1
    return out


def build_nc(phases=("A", "D", "B", "C"), debug=(), d_skip=()):
    nc = bass.Bass("TRN2", target_bir_lowering=False, detect_race_conditions=False)
    es = ExitStack()
    sm = Sems(nc, es)

    x_d   = nc.dram_tensor("x_d",   [S, DM], F32, kind="ExternalInput")
    wq_d  = nc.dram_tensor("wq_d",  [H * 128, DM], F32, kind="ExternalInput")
    wk_d  = nc.dram_tensor("wk_d",  [H * 128, DM], F32, kind="ExternalInput")
    wv_d  = nc.dram_tensor("wv_d",  [H * 128, DM], F32, kind="ExternalInput")
    wo_d  = nc.dram_tensor("wo_d",  [4 * 128, NC * 512], F32, kind="ExternalInput")
    wg0_d = nc.dram_tensor("wg0_d", [64 * 128, DM], F32, kind="ExternalInput")
    wg1_d = nc.dram_tensor("wg1_d", [64 * 128, DM], F32, kind="ExternalInput")
    wl_d  = nc.dram_tensor("wl_d",  [64 * 128, DM], F32, kind="ExternalInput")
    idf_d = nc.dram_tensor("idf_d", [128, 128], F32, kind="ExternalInput")
    idb_d = nc.dram_tensor("idb_d", [128, 128], BF16, kind="ExternalInput")
    msk_d = nc.dram_tensor("msk_d", [128, 128], BF16, kind="ExternalInput")
    eps_d = nc.dram_tensor("eps_d", [128, 1], F32, kind="ExternalInput")

    out_d = nc.dram_tensor("out_d", [S, DM], F32, kind="ExternalOutput")
    xn_h  = nc.dram_tensor("xn_h",  [NC * 128, S], BF16)
    y_h   = nc.dram_tensor("y_h",   [S, DM], F32)

    dbg = {}
    if "xnqx" in debug:
        dbg["xn_o"] = nc.dram_tensor("xn_o", [NC * 128, S], BF16, kind="ExternalOutput")
        dbg["qx_o"] = nc.dram_tensor("qx_o", [NC * 128, S], FP8, kind="ExternalOutput")
    if "y" in debug:
        dbg["y_o"] = nc.dram_tensor("y_o", [S, DM], F32, kind="ExternalOutput")
    if "stats" in debug:
        dbg["st_o"] = nc.dram_tensor("st_o", [128, 4 * NT], F32, kind="ExternalOutput")
    if "c" in debug:
        dbg["c_o"] = nc.dram_tensor("c_o", [128, NT * DM], BF16, kind="ExternalOutput")
        dbg["r_o"] = nc.dram_tensor("r_o", [128, H * NT], F32, kind="ExternalOutput")

    def mk_sb(stack):
        def f(name, shape, dt):
            return stack.enter_context(nc.sbuf_tensor(name, shape, dt))
        return f

    SBg = mk_sb(es)
    idf    = SBg("idf", [128, 128], F32)
    idb    = SBg("idb", [128, 128], BF16)
    msk    = SBg("msk", [128, 128], BF16)
    eps_ap = SBg("eps_ap", [128, 1], F32)

    with nc.Block(no_gpsimd_drain=True) as blk:
        @blk.sync
        def _(sync):
            sm.inc(sync.dma_start(idf[:, :], idf_d[:, :]), "misc", 16)
            sm.inc(sync.dma_start(idb[:, :], idb_d[:, :]), "misc", 16)
            sm.inc(sync.dma_start(msk[:, :], msk_d[:, :]), "misc", 16)
            sm.inc(sync.dma_start(eps_ap[:, :], eps_d[:, :]), "misc", 16)
            sync.wait_ge(sm.get("misc"), sm.val("misc"))

    esAD = ExitStack()
    SAD = mk_sb(esAD)
    qx_sb = SAD("qx_sb", [128, NC * S], FP8)
    wg0_t = [SAD(f"wg0_t{r}", [128, DM], FP8) for r in range(2)]
    wg1_t = [SAD(f"wg1_t{r}", [128, DM], FP8) for r in range(2)]

    if "A" in phases:
        esA = ExitStack()
        SA = mk_sb(esA)
        xbuf   = [SA(f"xbuf{i}", [128, DM], F32) for i in range(8)]
        xn_sbA = SA("xn_sbA", [128, NC * S], BF16)
        xntok  = [SA(f"xntok{r}", [128, DM], F32) for r in range(2)]
        sqscr  = SA("sqscr", [128, DM], F32)
        sumx   = SA("sumx", [128, NT], F32)
        sumsq  = SA("sumsq", [128, NT], F32)
        mm_    = SA("mm_", [128, NT], F32)
        vv_    = SA("vv_", [128, NT], F32)
        msq_   = SA("msq_", [128, NT], F32)
        vvm_   = SA("vvm_", [128, NT], F32)
        sd_    = SA("sd_", [128, NT], F32)
        rs_    = SA("rs_", [128, NT], F32)
        PT = [esA.enter_context(nc.psum_tensor(f"PTa{r}", [128, 128], F32)) for r in range(4)]

        with nc.Block(no_gpsimd_drain=True) as blk:
            @blk.sync
            def _(sync):
                for t in range(0, NT, 2):
                    i = t % 8
                    if t >= 8:
                        sync.wait_ge(sm.get("s_n"), t - 8 + 1)
                    sm.inc(sync.dma_start(xbuf[i][:, :], x_d[t * 128:(t + 1) * 128, :]), f"s_xb{i}", 16)
                sync.wait_ge(sm.get("s_xnb"), 256)
                if "stats" in debug:
                    for (bi, buf) in enumerate((sumx, sumsq, mm_, rs_)):
                        sm.inc(sync.dma_start(dbg["st_o"][:, bi * NT:(bi + 1) * NT], buf[:, :]), "s_dbgs", 16)
                for c in range(NC):
                    sm.inc(sync.dma_start(xn_h[c * 128:(c + 1) * 128, :], xn_sbA[:, c * S:(c + 1) * S]), "s_xn_out", 16)
                if "xnqx" in debug:
                    for c in range(NC):
                        sm.inc(sync.dma_start(dbg["xn_o"][c * 128:(c + 1) * 128, :], xn_sbA[:, c * S:(c + 1) * S]), "s_dbg", 16)
                    sync.wait_ge(sm.get("s_qx"), 256)
                    for c in range(NC):
                        sm.inc(sync.dma_start(dbg["qx_o"][c * 128:(c + 1) * 128, :], qx_sb[:, c * S:(c + 1) * S]), "s_dbg", 16)
                    sync.wait_ge(sm.get("s_dbg"), sm.val("s_dbg"))

            @blk.gpsimd
            def _(gpsimd):
                if "D" in phases:
                    for ht in range(2):
                        sm.inc(gpsimd.dma_start(wg0_t[ht % 2][:, :], wg0_d[ht * 128:(ht + 1) * 128, :]), f"s_wg0_{ht % 2}", 16)
                        sm.inc(gpsimd.dma_start(wg1_t[ht % 2][:, :], wg1_d[ht * 128:(ht + 1) * 128, :]), f"s_wg1_{ht % 2}", 16)

            @blk.vector
            def _(vector):
                def xn_evac(j):
                    for c in range(NC):
                        vector.wait_ge(sm.get("s_ta"), j * 16 + c + 1)
                        sm.inc(vector.tensor_copy(xn_sbA[:, c * S + j * 128: c * S + (j + 1) * 128],
                                                  PT[(j * 16 + c) % 4][:, :]), "s_xnb", 1)
                def norm(t, k):
                    if t >= 2:
                        vector.wait_ge(sm.get("s_ta"), (t - 1) * 16)
                    sm.inc(vector.tensor_scalar(xntok[t % 2][:, :], xbuf[k][:, :],
                                                mm_[:, t:t + 1], rs_[:, t:t + 1],
                                                op0=ALU.subtract, op1=ALU.mult), "s_n", 1)
                for b in range(2):
                    for k in range(8):
                        t = b * 8 + k
                        vector.wait_ge(sm.get(f"s_xb{k}"), 16 * (b + 1))
                        sm.inc(vector.reduce_sum(sumx[:, t:t + 1], xbuf[k][:, :], axis=AXL.X), "s_v1", 1)
                    cols = slice(b * 8, b * 8 + 8)
                    vector.wait_ge(sm.get("s_s1"), 8 * (b + 1))
                    vector.wait_ge(sm.get("s_v1"), 8 * (b + 1))
                    sm.inc(vector.tensor_scalar_mul(mm_[:, cols], sumx[:, cols], 1.0 / DM), "s_v2", 1)
                    sm.inc(vector.tensor_scalar_mul(vv_[:, cols], sumsq[:, cols], 1.0 / DM), "s_v5", 1)
                    vector.wait_ge(sm.get("s_s2"), b + 1)
                    vector.wait_ge(sm.get("s_v5"), b + 1)
                    sm.inc(vector.tensor_sub(vvm_[:, cols], vv_[:, cols], msq_[:, cols]), "s_v3", 1)
                    vector.wait_ge(sm.get("s_s3"), b + 1)
                    sm.inc(vector.reciprocal(rs_[:, cols], sd_[:, cols]), "s_v4", 1)
                    vector.wait_ge(sm.get("s_v4"), b + 1)
                    for k in range(8):
                        t = b * 8 + k
                        if t >= 1:
                            xn_evac(t - 1)
                        norm(t, k)
                xn_evac(NT - 1)

            @blk.scalar
            def _(scalar):
                def qx_evac(j):
                    for c in range(NC):
                        scalar.wait_ge(sm.get("s_ta"), j * 16 + c + 1)
                        scalar.wait_ge(sm.get("s_xnb"), j * 16 + c + 1)
                        sm.inc(scalar.copy(qx_sb[:, c * S + j * 128: c * S + (j + 1) * 128],
                                           PT[(j * 16 + c) % 4][:, :]), "s_qx", 1)
                def stats(b):
                    for k in range(8):
                        t = b * 8 + k
                        scalar.wait_ge(sm.get(f"s_xb{k}"), 16 * (b + 1))
                        sm.inc(scalar.activation(sqscr[:, :], xbuf[k][:, :], AF.Square,
                                                 accum_out=sumsq[:, t:t + 1]), "s_s1", 1)
                    cols = slice(b * 8, b * 8 + 8)
                    scalar.wait_ge(sm.get("s_v2"), b + 1)
                    sm.inc(scalar.activation(msq_[:, cols], mm_[:, cols], AF.Square), "s_s2", 1)
                    scalar.wait_ge(sm.get("s_v3"), b + 1)
                    sm.inc(scalar.activation(sd_[:, cols], vvm_[:, cols], AF.Sqrt, bias=eps_ap[:, :]), "s_s3", 1)
                for t in range(1, 8, 2):
                    sm.inc(scalar.dma_start(xbuf[t][:, :], x_d[t * 128:(t + 1) * 128, :]), f"s_xb{t}", 16)
                stats(0)
                for j in range(0, 7):
                    qx_evac(j)
                for t in range(9, NT, 2):
                    scalar.wait_ge(sm.get("s_n"), t - 8 + 1)
                    sm.inc(scalar.dma_start(xbuf[t % 8][:, :], x_d[t * 128:(t + 1) * 128, :]), f"s_xb{t % 8}", 16)
                stats(1)
                for j in range(7, NT):
                    qx_evac(j)

            @blk.tensor
            def _(tensor):
                tensor.wait_ge(sm.get("misc"), 64)
                for t in range(NT):
                    tensor.wait_ge(sm.get("s_n"), t + 1)
                    for c in range(NC):
                        k = t * 16 + c
                        if k >= 4:
                            tensor.wait_ge(sm.get("s_xnb"), k - 3)
                            tensor.wait_ge(sm.get("s_qx"), k - 3)
                        sm.inc(tensor.transpose(PT[k % 4][:, :], xntok[t % 2][:, c * 128:(c + 1) * 128],
                                                idf[:, :]), "s_ta", 1)
        esA.close()

    if "D" in phases:
        esD = ExitStack()
        SD = mk_sb(esD)
        act_sb = SD("act_sb", [128, QH * S], FP8)
        wl_sb  = SD("wl_sb", [128, QH * S], FP8)
        g_sb   = SD("g_sb", [128, DM], BF16)
        f1_sb  = SD("f1_sb", [128, DM], BF16)
        e1     = SD("e1", [128, DM], BF16)
        e2     = SD("e2", [128, DM], BF16)
        gel    = SD("gel", [128, DM], BF16)
        xy     = [SD(f"xy{r}", [128, DM], F32) for r in range(2)]
        yo     = [SD(f"yo{r}", [128, DM], F32) for r in range(2)]
        PA = esD.enter_context(nc.psum_tensor("PA", [128, DM], F32))
        PB = esD.enter_context(nc.psum_tensor("PB", [128, DM], F32))

        with nc.Block(no_gpsimd_drain=True) as blk:
            @blk.sync
            def _(sync):
                if "A" in phases:
                    sync.wait_ge(sm.get("s_xn_out"), 256)
                for q in range(NQ):
                    for i in range(NT):
                        if "d2" in d_skip:
                            break
                        n = q * NT + i
                        if n >= 2:
                            sync.wait_ge(sm.get("s_ya"), 4 * (n - 1))
                        src = x_d if q == 0 else y_h
                        sm.inc(sync.dma_start(xy[i % 2][:, :], src[i * 128:(i + 1) * 128, :]), f"s_xy{n % 2}", 16)
                        sync.wait_ge(sm.get("s_ya"), 4 * (n + 1))
                        sm.inc(sync.dma_start(y_h[i * 128:(i + 1) * 128, :], yo[i % 2][:, :]), f"s_yw{n % 2}", 16)
                if "y" in debug and "d2" not in d_skip:
                    sync.wait_ge(sm.get("s_yw0"), 16 * NQ * NT // 2)
                    sync.wait_ge(sm.get("s_yw1"), 16 * NQ * NT // 2)
                    for i in range(NT):
                        sm.inc(sync.dma_start(dbg["y_o"][i * 128:(i + 1) * 128, :], y_h[i * 128:(i + 1) * 128, :]), "s_dbg2", 16)
                    sync.wait_ge(sm.get("s_dbg2"), sm.val("s_dbg2"))
                if "C" not in phases:
                    if "d2" not in d_skip:
                        sync.wait_ge(sm.get("s_yw0"), 16 * NQ * NT // 2)
                        sync.wait_ge(sm.get("s_yw1"), 16 * NQ * NT // 2)
                    for i in range(NT):
                        sm.inc(sync.dma_start(out_d[i * 128:(i + 1) * 128, :], y_h[i * 128:(i + 1) * 128, :]), "s_fin", 16)
                    sync.wait_ge(sm.get("s_fin"), sm.val("s_fin"))

            @blk.gpsimd
            def _(gpsimd):
                if "A" in phases:
                    gpsimd.wait_ge(sm.get("s_xn_out"), 256)
                pre = 2 if "A" in phases else 0
                def wg_load(n):
                    if n >= pre and n < NQ * QH:
                        if n >= 2:
                            gpsimd.wait_ge(sm.get("s_gf"), 2 * (n - 1))
                        sm.inc(gpsimd.dma_start(wg0_t[n % 2][:, :], wg0_d[n * 128:(n + 1) * 128, :]), f"s_wg0_{n % 2}", 16)
                        sm.inc(gpsimd.dma_start(wg1_t[n % 2][:, :], wg1_d[n * 128:(n + 1) * 128, :]), f"s_wg1_{n % 2}", 16)
                for q in range(NQ):
                    for ht in range(QH):
                        wg_load(q * QH + ht)
                        if "d2" not in d_skip:
                            row = q * QH + ht
                            if q >= 1:
                                gpsimd.wait_ge(sm.get("s_d2"), 4 * NT * q)
                            sm.inc(gpsimd.dma_start(wl_sb[:, ht * S:(ht + 1) * S],
                                                    wl_d[row * 128:(row + 1) * 128, :]), f"s_wl{q % 2}", 16)

            @blk.tensor
            def _(tensor):
                if "A" in phases:
                    tensor.wait_ge(sm.get("s_qx"), 256)
                for q in range(NQ):
                    if q >= 1:
                        tensor.wait_ge(sm.get("s_ya"), 64 * q - 4)
                    for ht in range(QH):
                        n = q * QH + ht
                        tensor.wait_ge(sm.get(f"s_wg0_{n % 2}"), 16 * (n // 2 + 1))
                        tensor.wait_ge(sm.get(f"s_wg1_{n % 2}"), 16 * (n // 2 + 1))
                        if n >= 1:
                            tensor.wait_ge(sm.get("s_gev"), 2 * n - 1)
                        for tc in range(4):
                            for c2 in range(NC // 2):
                                mi = tensor.matmul(PA[:, tc * 512:(tc + 1) * 512],
                                              bass.AP(wg0_t[n % 2], 2 * c2 * 128, [[S, 128], [128, 2], [1, 128]]),
                                              bass.AP(qx_sb, 2 * c2 * S + tc * 512, [[NC * S, 128], [S, 2], [1, 512]]),
                                              start=(c2 == 0), stop=(c2 == NC // 2 - 1), perf_mode=DR)
                        sm.inc(mi, "s_gf", 1)
                        if n >= 1:
                            tensor.wait_ge(sm.get("s_gev"), 2 * n)
                        if q >= 1 and ht == 0:
                            tensor.wait_ge(sm.get("s_ya"), 64 * q)
                        for tc in range(4):
                            for c2 in range(NC // 2):
                                mi = tensor.matmul(PB[:, tc * 512:(tc + 1) * 512],
                                              bass.AP(wg1_t[n % 2], 2 * c2 * 128, [[S, 128], [128, 2], [1, 128]]),
                                              bass.AP(qx_sb, 2 * c2 * S + tc * 512, [[NC * S, 128], [S, 2], [1, 512]]),
                                              start=(c2 == 0), stop=(c2 == NC // 2 - 1), perf_mode=DR)
                        sm.inc(mi, "s_gf", 1)
                    if "d2" in d_skip:
                        continue
                    tensor.wait_ge(sm.get("s_act"), (q + 1) * QH)
                    tensor.wait_ge(sm.get(f"s_wl{q % 2}"), 16 * QH * (q // 2 + 1))
                    tensor.wait_ge(sm.get("s_gev"), 2 * QH * (q + 1))
                    for i in range(NT):
                        n2 = q * NT + i
                        P = PA if i % 2 == 0 else PB
                        if n2 >= 2:
                            tensor.wait_ge(sm.get("s_ya"), 4 * (n2 - 1))
                        for fc in range(4):
                            for h2 in range(QH // 2):
                                mi = tensor.matmul(P[:, fc * 512:(fc + 1) * 512],
                                              bass.AP(act_sb, 2 * h2 * S + i * 128, [[QH * S, 128], [S, 2], [1, 128]]),
                                              bass.AP(wl_sb, 2 * h2 * S + fc * 512, [[QH * S, 128], [S, 2], [1, 512]]),
                                              start=(h2 == 0), stop=(h2 == QH // 2 - 1), perf_mode=DR)
                            sm.inc(mi, "s_d2", 1)

            @blk.vector
            def _(vector):
                for q in range(NQ):
                    if q >= 1 and "d2" not in d_skip:
                        vector.wait_ge(sm.get("s_d2"), 4 * NT * q)
                    for ht in range(QH):
                        n = q * QH + ht
                        vector.wait_ge(sm.get("s_gev"), 2 * n + 1)
                        vector.tensor_mul(e1[:, :], g_sb[:, :], g_sb[:, :])
                        vector.tensor_mul(e2[:, :], e1[:, :], g_sb[:, :])
                        vector.tensor_scalar_mul(e1[:, :], e2[:, :], C2)
                        vector.tensor_add(e2[:, :], g_sb[:, :], e1[:, :])
                        sm.inc(vector.tensor_scalar_mul(e1[:, :], e2[:, :], C1), "s_w", 1)
                        vector.wait_ge(sm.get("s_th"), n + 1)
                        vector.wait_ge(sm.get("s_gev"), 2 * n + 2)
                        vector.tensor_scalar(e2[:, :], gel[:, :], 1.0, 0.5, op0=ALU.add, op1=ALU.mult)
                        vector.tensor_mul(e1[:, :], g_sb[:, :], e2[:, :])
                        vector.tensor_mul(e2[:, :], e1[:, :], f1_sb[:, :])
                        sm.inc(vector.tensor_copy(act_sb[:, ht * S:(ht + 1) * S], e2[:, :]), "s_act", 1)
                    for i in range(NT):
                        if "d2" in d_skip:
                            break
                        n2 = q * NT + i
                        P = PA if i % 2 == 0 else PB
                        vector.wait_ge(sm.get(f"s_xy{n2 % 2}"), 16 * (n2 // 2 + 1))
                        if n2 >= 2:
                            vector.wait_ge(sm.get(f"s_yw{n2 % 2}"), 16 * (n2 // 2))
                        for fc in range(4):
                            vector.wait_ge(sm.get("s_d2"), 4 * n2 + fc + 1)
                            sm.inc(vector.tensor_add(yo[i % 2][:, fc * 512:(fc + 1) * 512],
                                                     P[:, fc * 512:(fc + 1) * 512],
                                                     xy[i % 2][:, fc * 512:(fc + 1) * 512]), "s_ya", 1)

            @blk.scalar
            def _(scalar):
                for q in range(NQ):
                    for ht in range(QH):
                        n = q * QH + ht
                        scalar.wait_ge(sm.get("s_gf"), 2 * n + 1)
                        if n >= 1:
                            scalar.wait_ge(sm.get("s_act"), n)
                        sm.inc(scalar.copy(g_sb[:, :], PA[:, :]), "s_gev", 1)
                        scalar.wait_ge(sm.get("s_gf"), 2 * n + 2)
                        sm.inc(scalar.copy(f1_sb[:, :], PB[:, :]), "s_gev", 1)
                        scalar.wait_ge(sm.get("s_w"), n + 1)
                        sm.inc(scalar.activation(gel[:, :], e1[:, :], AF.Tanh), "s_th", 1)
        esD.close()
    esAD.close()

    if "B" in phases or "C" in phases:
        esBC = ExitStack()
        SBC = mk_sb(esBC)
        c_sb = SBC("c_sb", [128, NT * DM], BF16)
        r_sb = SBC("r_sb", [128, H * NT], F32)

    if "B" in phases:
        esB = ExitStack()
        SBB = mk_sb(esB)
        xn_sbB = SBB("xn_sbB", [128, NC * S], BF16)
        q_sbh  = [SBB(f"q_sbh{r}", [128, S], BF16) for r in range(2)]
        k_sbh  = [SBB(f"k_sbh{r}", [128, S], BF16) for r in range(2)]
        vt_sbh = [SBB(f"vt_sbh{r}", [128, S], BF16) for r in range(2)]
        v_sbh  = [SBB(f"v_sbh{r}", [128, NT * 192], BF16) for r in range(2)]
        pt_sb  = SBB("pt_sb", [128, NT * 512], BF16)
        wqt = [SBB(f"wqt{r}", [128, S], BF16) for r in range(2)]
        wkt = [SBB(f"wkt{r}", [128, S], BF16) for r in range(2)]
        wvt = [SBB(f"wvt{r}", [128, S], BF16) for r in range(2)]
        lnS = SBB("lnS", [128, NT], F32)
        PP  = [esB.enter_context(nc.psum_tensor(f"PPb{r}", [128, 512], F32)) for r in range(2)]
        PS_ = [esB.enter_context(nc.psum_tensor(f"PSb{r}", [128, 1024], F32)) for r in range(2)]
        PC_ = [esB.enter_context(nc.psum_tensor(f"PCb{r}", [128, 132], F32)) for r in range(2)]
        

        ST = score_tiles()
        NSC = len(ST)
        CPH = 16

        with nc.Block(no_gpsimd_drain=True) as blk:
            @blk.sync
            def _(sync):
                for c in range(NC):
                    sm.inc(sync.dma_start(xn_sbB[:, c * S:(c + 1) * S], xn_h[c * 128:(c + 1) * 128, :]), f"s_xni{c // 4}", 16)
                for h in range(H):
                    sync.wait_ge(sm.get("s_pe"), 12 * (h + 1))
                    if h >= 2:
                        sync.wait_ge(sm.get("s_cm"), (h - 1) * CPH)
                    for jt in range(NT):
                        sm.inc(sync.dma_start_transpose(v_sbh[h % 2][:, jt * 192: jt * 192 + 128],
                                                        vt_sbh[h % 2][:, jt * 128:(jt + 1) * 128]), f"s_vtt{h % 2}", 16)
                if "c" in debug:
                    sync.wait_ge(sm.get("s_r"), H)
                    sync.wait_ge(sm.get("s_cev"), 2 * H * NT)
                    sm.inc(sync.dma_start(dbg["c_o"][:, :], c_sb[:, :]), "s_dbg3", 16)
                    sm.inc(sync.dma_start(dbg["r_o"][:, :], r_sb[:, :]), "s_dbg3", 16)
                    sync.wait_ge(sm.get("s_dbg3"), sm.val("s_dbg3"))
                if "C" not in phases:
                    sync.wait_ge(sm.get("s_r"), H)
                    for i in range(NT):
                        sm.inc(sync.dma_start(out_d[i * 128:(i + 1) * 128, :], y_h[i * 128:(i + 1) * 128, :]), "s_fin", 16)
                    sync.wait_ge(sm.get("s_fin"), sm.val("s_fin"))

            @blk.gpsimd
            def _(gpsimd):
                for h in range(H):
                    if h >= 2:
                        gpsimd.wait_ge(sm.get("s_pj"), 12 * (h - 1))
                    sm.inc(gpsimd.dma_start(wqt[h % 2][:, :], wq_d[h * 128:(h + 1) * 128, :]), f"s_wq{h % 2}", 16)
                    sm.inc(gpsimd.dma_start(wkt[h % 2][:, :], wk_d[h * 128:(h + 1) * 128, :]), f"s_wk{h % 2}", 16)
                    sm.inc(gpsimd.dma_start(wvt[h % 2][:, :], wv_d[h * 128:(h + 1) * 128, :]), f"s_wv{h % 2}", 16)

            @blk.tensor
            def _(tensor):
                tensor.wait_ge(sm.get("misc"), 64)
                tensor.wait_ge(sm.get("s_vone"), 2)
                nsc = 0; ncc = 0; npj = [0]

                def proj_chain(hh, k):
                    pi, tc = divmod(k, 4)
                    wt = (wqt, wkt, wvt)[pi]
                    if k == 0:
                        tensor.wait_ge(sm.get(f"s_wq{hh % 2}"), 16 * (hh // 2 + 1))
                        tensor.wait_ge(sm.get(f"s_wk{hh % 2}"), 16 * (hh // 2 + 1))
                        tensor.wait_ge(sm.get(f"s_wv{hh % 2}"), 16 * (hh // 2 + 1))
                    if npj[0] >= 2:
                        tensor.wait_ge(sm.get("s_pe"), npj[0] - 1)
                    for c in range(NC):
                        if npj[0] == 0:
                            tensor.wait_ge(sm.get(f"s_xni{c // 4}"), 64)
                        mi = tensor.matmul(PP[npj[0] % 2][:, :],
                                      wt[hh % 2][:, c * 128:(c + 1) * 128],
                                      xn_sbB[:, c * S + tc * 512: c * S + (tc + 1) * 512],
                                      start=(c == 0), stop=(c == NC - 1))
                    sm.inc(mi, "s_pj", 1)
                    npj[0] += 1

                for k in range(12):
                    proj_chain(0, k)
                npr = 0
                for h in range(H):
                    for Bi in range(4):
                        Jmax = 4 * Bi + 4
                        kpos = {max(1, Jmax * (k + 1) // 4): 3 * Bi + k for k in range(3)}
                        for J in range(Jmax):
                            o = J * 128 - Bi * 512
                            if J % 2 == 0 and npr >= 2:
                                tensor.wait_ge(sm.get("s_exp"), npr - 1)
                            mi = tensor.matmul(PS_[npr % 2][:, (J % 2) * 512: (J % 2) * 512 + 512],
                                          k_sbh[h % 2][:, J * 128:(J + 1) * 128],
                                          q_sbh[h % 2][:, Bi * 512: Bi * 512 + 512],
                                          start=True, stop=True)
                            sm.inc(mi, "s_sc", 1)
                            nsc += 1
                            if J % 2 == 1:
                                npr += 1
                            if h + 1 < H and (J + 1) in kpos:
                                proj_chain(h + 1, kpos[J + 1])
                        tensor.wait_ge(sm.get("s_exp"), npr)
                        tensor.wait_ge(sm.get(f"s_vtt{h % 2}"), 256 * (h // 2 + 1))
                        for t4 in range(4):
                            it = 4 * Bi + t4
                            if ncc >= 2:
                                tensor.wait_ge(sm.get("s_cev"), 2 * (ncc - 1))
                            for J in range(it + 1):
                                mi = tensor.matmul(PC_[ncc % 2][:, 0:129],
                                              pt_sb[:, J * 512 + t4 * 128: J * 512 + (t4 + 1) * 128],
                                              v_sbh[h % 2][:, J * 192: J * 192 + 129],
                                              start=(J == 0), stop=(J == it))
                            sm.inc(mi, "s_cm", 1)
                            ncc += 1

            @blk.vector
            def _(vector):
                for r in range(2):
                    for jt in range(NT):
                        mi = vector.memset(v_sbh[r][:, jt * 192 + 128: jt * 192 + 129], 1.0)
                    sm.inc(mi, "s_vone", 1)
                npj = 0; nvt = 0

                def proj_evac(hh, k):
                    nonlocal npj
                    pi = k // 4
                    dst = (q_sbh, k_sbh, vt_sbh)[pi][hh % 2]
                    if hh >= 2:
                        if k == 0:
                            vector.wait_ge(sm.get("s_sc"), (hh - 1) * NSC)
                        if k == 8:
                            vector.wait_ge(sm.get(f"s_vtt{hh % 2}"), 256 * (hh // 2))
                    vector.wait_ge(sm.get("s_pj"), npj + 1)
                    sm.inc(vector.tensor_copy(dst[:, (k % 4) * 512:(k % 4 + 1) * 512], PP[npj % 2][:, :]), "s_pe", 1)
                    npj += 1

                for k in range(12):
                    proj_evac(0, k)
                nprv = 0
                for h in range(H):
                    for Bi in range(4):
                        Jmax = 4 * Bi + 4
                        kpos = {max(1, Jmax * (k + 1) // 4): 3 * Bi + k for k in range(3)}
                        base = sum(4 * b + 4 for b in range(Bi))
                        for J in range(Jmax):
                            o = J * 128 - Bi * 512
                            if o >= 0:
                                g = h * NSC + base + J
                                vector.wait_ge(sm.get("s_sc"), g + 1)
                                sm.inc(vector.tensor_tensor(PS_[nprv % 2][:, (J % 2) * 512 + o: (J % 2) * 512 + o + 128],
                                                            PS_[nprv % 2][:, (J % 2) * 512 + o: (J % 2) * 512 + o + 128],
                                                            msk[:, :], op=ALU.add), "s_mk", 1)
                            if J % 2 == 1:
                                nprv += 1
                            if h + 1 < H and (J + 1) in kpos:
                                proj_evac(h + 1, kpos[J + 1])

            @blk.scalar
            def _(scalar):
                nmk = 0; nprs = 0
                for h in range(H):
                    for Bi in range(4):
                        if h > 0 or Bi > 0:
                            scalar.wait_ge(sm.get("s_cm"), h * CPH + 4 * Bi)
                        for Jl in range(0, 4 * Bi + 4, 2):
                            zero_after = []
                            for J in (Jl, Jl + 1):
                                o = J * 128 - Bi * 512
                                idx = sum(4 * b + 4 for b in range(Bi)) + J
                                g = h * NSC + idx
                                scalar.wait_ge(sm.get("s_sc"), g + 1)
                                if o >= 0:
                                    nmk += 1
                                    scalar.wait_ge(sm.get("s_mk"), nmk)
                                    if o > 0:
                                        zero_after.append((J, o))
                            sm.inc(scalar.activation(pt_sb[:, Jl * 512: (Jl + 2) * 512],
                                                     PS_[nprs % 2][:, 0:1024], AF.Exp,
                                                     scale=INVSQRT_HD), "s_exp", 1)
                            nprs += 1
                            for (J, o) in zero_after:
                                scalar.memzero(pt_sb[:, J * 512: J * 512 + o])
                            if zero_after:
                                scalar.wait_ge(sm.get("s_exp"), nprs)
                        for t4 in range(4):
                            it = 4 * Bi + t4
                            cidx = h * CPH + it
                            scalar.wait_ge(sm.get("s_cm"), cidx + 1)
                            sm.inc(scalar.activation(lnS[:, it:it + 1], PC_[cidx % 2][:, 128:129], AF.Ln), "s_cev", 1)
                            sm.inc(scalar.copy(c_sb[:, it * DM + h * 128: it * DM + (h + 1) * 128],
                                               PC_[cidx % 2][:, 0:128]), "s_cev", 1)
                    scalar.wait_ge(sm.get("s_cev"), 2 * (h + 1) * NT)
                    sm.inc(scalar.activation(r_sb[:, h * NT:(h + 1) * NT], lnS[:, :], AF.Exp, scale=-1.0), "s_r", 1)
        esB.close()

    if "C" in phases:
        esC = ExitStack()
        SC = mk_sb(esC)
        cTt = [SC(f"cTt{r}", [128, DM], BF16) for r in range(2)]
        wo_sb = SC("wo_sb", [128, NC * S], BF16)
        ys = [SC(f"ys{r}", [128, DM], F32) for r in range(2)]
        os_ = [SC(f"os{r}", [128, DM], F32) for r in range(2)]
        PTC = [esC.enter_context(nc.psum_tensor(f"PTc{r}", [128, 128], BF16)) for r in range(2)]
        PO  = [esC.enter_context(nc.psum_tensor(f"POc{r}", [128, 512], F32)) for r in range(4)]

        with nc.Block(no_gpsimd_drain=True) as blk:
            @blk.sync
            def _(sync):
                for i in range(NT):
                    if i >= 2:
                        sync.wait_ge(sm.get("s_oa"), 4 * (i - 1))
                    sm.inc(sync.dma_start(ys[i % 2][:, :], y_h[i * 128:(i + 1) * 128, :]), f"s_yc{i % 2}", 16)
                sync.wait_ge(sm.get("s_out0"), 16 * NT // 2)
                sync.wait_ge(sm.get("s_out1"), 16 * NT // 2)

            @blk.scalar
            def _(scalar):
                for i in range(NT):
                    scalar.wait_ge(sm.get("s_oa"), 4 * (i + 1))
                    sm.inc(scalar.dma_start(out_d[i * 128:(i + 1) * 128, :], os_[i % 2][:, :]), f"s_out{i % 2}", 16)
                scalar.wait_ge(sm.get("s_out0"), 16 * NT // 2)
                scalar.wait_ge(sm.get("s_out1"), 16 * NT // 2)

            @blk.gpsimd
            def _(gpsimd):
                for fc in range(4):
                    sm.inc(gpsimd.dma_start(wo_sb[:, fc * NC * 512:(fc + 1) * NC * 512],
                                            wo_d[fc * 128:(fc + 1) * 128, :]), "s_wo", 16)

            @blk.vector
            def _(vector):
                for i in range(NT):
                    for hh in range(H):
                        mi = vector.tensor_scalar_mul(c_sb[:, i * DM + hh * 128: i * DM + (hh + 1) * 128],
                                                      c_sb[:, i * DM + hh * 128: i * DM + (hh + 1) * 128],
                                                      r_sb[:, hh * NT + i: hh * NT + i + 1])
                    sm.inc(mi, "s_csc", 1)
                for i in range(NT):
                    if i >= 2:
                        vector.wait_ge(sm.get("s_po"), 4 * (i - 1))
                    for dc in range(NC):
                        ntc = i * NC + dc
                        vector.wait_ge(sm.get("s_tc"), ntc + 1)
                        sm.inc(vector.tensor_copy(cTt[i % 2][:, dc * 128:(dc + 1) * 128], PTC[ntc % 2][:, :]), "s_cte", 1)
                    for fc in range(4):
                        npo = 4 * i + fc
                        vector.wait_ge(sm.get("s_po"), npo + 1)
                        vector.wait_ge(sm.get(f"s_yc{i % 2}"), 16 * (i // 2 + 1))
                        if i >= 2:
                            vector.wait_ge(sm.get(f"s_out{i % 2}"), 16 * (i // 2))
                        sm.inc(vector.tensor_add(os_[i % 2][:, fc * 512:(fc + 1) * 512],
                                                 PO[npo % 4][:, :], ys[i % 2][:, fc * 512:(fc + 1) * 512]), "s_oa", 1)

            @blk.tensor
            def _(tensor):
                for i in range(NT):
                    tensor.wait_ge(sm.get("s_csc"), i + 1)
                    for dc in range(NC):
                        ntc = i * NC + dc
                        if ntc >= 2:
                            tensor.wait_ge(sm.get("s_cte"), ntc - 1)
                        sm.inc(tensor.transpose(PTC[ntc % 2][:, :],
                                                c_sb[:, i * DM + dc * 128: i * DM + (dc + 1) * 128],
                                                idb[:, :]), "s_tc", 1)
                    tensor.wait_ge(sm.get("s_cte"), NC * (i + 1))
                    for fc in range(4):
                        npo = 4 * i + fc
                        if i == 0:
                            tensor.wait_ge(sm.get("s_wo"), 16 * (fc + 1))
                        if npo >= 4:
                            tensor.wait_ge(sm.get("s_oa"), npo - 3)
                        for dc in range(NC):
                            mi = tensor.matmul(PO[npo % 4][:, :],
                                          cTt[i % 2][:, dc * 128:(dc + 1) * 128],
                                          wo_sb[:, fc * NC * 512 + dc * 512: fc * NC * 512 + (dc + 1) * 512],
                                          start=(dc == 0), stop=(dc == NC - 1))
                        sm.inc(mi, "s_po", 1)
        esC.close()

    if "B" in phases or "C" in phases:
        esBC.close()

    if "D" not in phases and "B" not in phases and "C" not in phases:
        with nc.Block(no_gpsimd_drain=True) as blk:
            @blk.sync
            def _(sync):
                for i in range(NT):
                    sm.inc(sync.dma_start(out_d[i * 128:(i + 1) * 128, :], x_d[i * 128:(i + 1) * 128, :]), "s_fin", 16)
                sync.wait_ge(sm.get("s_fin"), sm.val("s_fin"))

    return nc, dbg


def prep_inputs(x_full, Wq, Wk, Wv, Wo, wg, wl):
    def qkv_img(W):
        A = np.ascontiguousarray(W.reshape(H, 128, NC, 128).transpose(0, 3, 2, 1))
        return A.reshape(H * 128, DM)
    wq_i = qkv_img(Wq); wk_i = qkv_img(Wk); wv_i = qkv_img(Wv)
    wo_i = np.ascontiguousarray(Wo.T.reshape(NC, 128, 4, 512).transpose(2, 1, 0, 3)).reshape(4 * 128, NC * 512)
    def wg_img(W):
        A = np.ascontiguousarray(W.reshape(NC, 128, 64, 128).transpose(2, 1, 0, 3))
        return A.reshape(64 * 128, DM)
    wg0_i = wg_img(wg[0]); wg1_i = wg_img(wg[1])
    wl_i = np.ascontiguousarray(wl)
    idf = np.eye(128, dtype=np.float32)
    idb = np.eye(128).astype(ml_dtypes.bfloat16)
    mvals = np.where(np.arange(128)[:, None] > np.arange(128)[None, :], -1e9, 0.0)
    mskv = mvals.astype(ml_dtypes.bfloat16)
    epsv = np.full((128, 1), 1e-5, np.float32)
    shared = dict(wq_d=wq_i, wk_d=wk_i, wv_d=wv_i, wo_d=wo_i,
                  wg0_d=wg0_i, wg1_d=wg1_i, wl_d=wl_i,
                  idf_d=idf, idb_d=idb, msk_d=mskv, eps_d=epsv)
    maps = []
    for b in range(x_full.shape[0]):
        m = dict(shared)
        m["x_d"] = np.ascontiguousarray(x_full[b])
        maps.append(m)
    return maps

from concourse.bass_utils import run_bass_kernel_spmd

_NC_CACHE = {}

def _get_nc():
    if "nc" not in _NC_CACHE:
        _NC_CACHE["nc"] = build_nc(phases=("A", "D", "B", "C"))[0]
    return _NC_CACHE["nc"]


def kernel(L0_attn_x, L0_attn_Wq, L0_attn_Wk, L0_attn_Wv, L0_attn_Wo,
           L0_ffn_w_gating, L0_ffn_w_linear):
    x = np.asarray(L0_attn_x, dtype=np.float32)
    nc = _get_nc()
    maps = prep_inputs(x,
                       np.asarray(L0_attn_Wq, dtype=np.float32),
                       np.asarray(L0_attn_Wk, dtype=np.float32),
                       np.asarray(L0_attn_Wv, dtype=np.float32),
                       np.asarray(L0_attn_Wo, dtype=np.float32),
                       np.asarray(L0_ffn_w_gating, dtype=np.float32),
                       np.asarray(L0_ffn_w_linear, dtype=np.float32))
    res = run_bass_kernel_spmd(nc, maps, core_ids=list(range(8)))
    return np.stack([np.asarray(res.results[b]["out_d"], dtype=np.float32) for b in range(8)], axis=0)
